# version 02
# baseline (speedup 1.0000x reference)
import os
import numpy as np
import ml_dtypes
import concourse.bass as bass
import concourse.mybir as mybir
from concourse.bass_utils import run_bass_kernel_spmd
from contextlib import ExitStack

F32 = mybir.dt.float32
BF16 = mybir.dt.bfloat16
I32 = mybir.dt.int32
U8 = mybir.dt.uint8
AF = mybir.ActivationFunctionType
ALU = mybir.AluOpType
AX = mybir.AxisListType.X
BF = ml_dtypes.bfloat16

NCORE = 8
S_ALL = 8192
E = 2048
TOK = 1024
NTT = 8
DA = 128
DB = 256
HFF = 8192
SCALE_A = 1.0 / float(np.sqrt(DA))
INV_SQ_E = 1.0 / float(np.sqrt(E))
LN_EPS = 1e-5
RMS_EPS = float(np.finfo(np.float32).eps)
THR_C = 0.50048828125
NEG_BIG = -1.0e9

ENGS = ("sync", "gpsimd", "vector", "scalar", "tensor")
DBG_PHASE = int(os.environ.get("MK_DEBUG_PHASE", "99"))
DBG_TERN = int(os.environ.get("MK_DEBUG_TERN", "0"))
DBG_DUMP = os.environ.get("MK_DEBUG_DUMP", "")

O_XT = 0
O_JK = 16384
O_ATT = 32768
O_XLB = 40960
O_C = 49152
O_KV = 126976


class Ctx:
    def __init__(self, nc, t, sems, wname, weng):
        self.nc, self.t, self.sem = nc, t, sems
        self.wname, self.weng = wname, weng
        self.cnt = {k: 0 for k in sems}
        self.waited = {k: 0 for k in sems}
        self.misc = {}

    def op(self, eng, fn, wait=(), inc=None):
        mine = (eng == self.wname)
        if mine:
            for tk in wait:
                if tk is None:
                    continue
                s, v = tk
                if v > self.waited[s]:
                    self.weng.wait_ge(self.sem[s], v)
                    self.waited[s] = v
        inst = fn(self.weng) if mine else None
        if inc is not None:
            s, n = inc
            self.cnt[s] += n
            if mine:
                inst.then_inc(self.sem[s], n)
            return (s, self.cnt[s])
        return None

    def dma(self, q, out, in_, wait=(), **kw):
        s = {"sync": "ds", "gpsimd": "dg", "scalar": "dc"}[q]
        return self.op(q, lambda e: e.dma_start(out=out, in_=in_, **kw), wait, (s, 16))

    def dmat(self, q, out, in_, wait=()):
        s = {"sync": "ds", "scalar": "dc"}[q]
        return self.op(q, lambda e: e.dma_start(out=out, in_=in_, transpose=True),
                       wait, (s, 16))

    def idma(self, out, in_, idx, wait=()):
        return self.op("gpsimd",
                       lambda e: e.indirect_dma_start(
                           out, None, in_, bass.IndirectOffsetOnAxis(ap=idx, axis=0)),
                       wait, ("dg", 16))

    def gather(self, in_ap, out_ap, wait=()):
        return self.op("gpsimd",
                       lambda e: e.collective_compute(
                           "AllGather", ALU.bypass, [list(range(NCORE))],
                           [in_ap], [out_ap]),
                       wait, ("cc", 1))

    def mm(self, out, lhsT, rhs, start, stop, wait=(), inc=False):
        return self.op("tensor",
                       lambda e: e.matmul(out, lhsT, rhs, start=start, stop=stop),
                       wait, ("pe", 1) if inc else None)

    def tr(self, out, in_, ident, wait=(), inc=False):
        return self.op("tensor", lambda e: e.transpose(out, in_, ident),
                       wait, ("pe", 1) if inc else None)

    def act(self, fn, wait=()):
        return self.op("scalar", fn, wait, ("ac", 1))

    def dve(self, fn, wait=()):
        return self.op("vector", fn, wait, ("dv", 1))

    def pool(self, fn, wait=()):
        return self.op("gpsimd", fn, wait, ("po", 1))

    def touch(self, eng, wait=()):
        j = self.t["junk"][:, 0:64]
        j2 = self.t["junk2"][:, 0:64]
        if eng == "scalar":
            return self.act(lambda e: e.activation(j2, j, AF.Copy), wait)
        return self.dve(lambda e: e.tensor_copy(j2, j), wait)


def make_tensors(nc, st):
    t = {}
    dram = lambda n, s, d, k="Internal": nc.dram_tensor(n, s, d, kind=k).ap()
    t["x_sh"] = dram("x_sh", [TOK, E], F32, "ExternalInput")
    t["wq_h"] = dram("wq_h", [2, E, DA], F32, "ExternalInput")
    t["wk_g"] = dram("wk_g", [E, DA], F32, "ExternalInput")
    t["wv_g"] = dram("wv_g", [E, DA], F32, "ExternalInput")
    t["bq_h"] = dram("bq_h", [2, DA], F32, "ExternalInput")
    t["bkv_g"] = dram("bkv_g", [2, DA], F32, "ExternalInput")
    t["wqkv_h"] = dram("wqkv_h", [E, 3 * DB], F32, "ExternalInput")
    t["bqkv_h"] = dram("bqkv_h", [3 * DB], F32, "ExternalInput")
    t["wproj"] = dram("wproj", [E, E], F32, "ExternalInput")
    t["bproj"] = dram("bproj", [E], F32, "ExternalInput")
    t["wgu_sh"] = dram("wgu_sh", [E, E], F32, "ExternalInput")
    t["wpr_sh"] = dram("wpr_sh", [256, HFF], F32, "ExternalInput")
    t["cst_bf"] = dram("cst_bf", [128, 768], BF16, "ExternalInput")
    t["cst_f"] = dram("cst_f", [128, 256], F32, "ExternalInput")
    t["idxa"] = dram("idxa", [128, 64], I32, "ExternalInput")
    t["idxb"] = dram("idxb", [128, 16], I32, "ExternalInput")
    t["out_sh"] = dram("out_sh", [TOK, E], F32, "ExternalOutput")
    t["resid"] = dram("resid", [TOK, E], F32)
    t["xln_c"] = dram("xln_c", [TOK, E], BF16)
    t["xln_loc"] = dram("xln_loc", [TOK, E], BF16)
    t["g_xa"] = dram("g_xa", [S_ALL, E], BF16)
    t["g_xb"] = dram("g_xb", [S_ALL, E], BF16)
    t["q_a"] = dram("q_a", [2 * DA, S_ALL], BF16)
    t["q_b"] = dram("q_b", [DB, S_ALL], BF16)
    t["aa_c"] = dram("aa_c", [S_ALL, 2 * DA], BF16)
    t["g_aa"] = dram("g_aa", [8 * S_ALL, 2 * DA], BF16)
    t["ab_c"] = dram("ab_c", [8 * DB, TOK], BF16)
    t["g_ab"] = dram("g_ab", [8 * 8 * DB, TOK], BF16)
    t["wgu_c"] = dram("wgu_c", [E, E], BF16)
    t["g_wgu"] = dram("g_wgu", [8 * E, E], BF16)
    t["wpr_c"] = dram("wpr_c", [HFF, 256], BF16)
    t["g_wpr"] = dram("g_wpr", [8 * HFF, 256], BF16)
    t["h_d"] = dram("h_d", [HFF, TOK], BF16)
    t["bounce"] = dram("bounce", [1, 2048], F32)
    sb = lambda n, s, d: st.enter_context(nc.sbuf_tensor(n, s, d)).ap()
    t["cbf"] = sb("cbf", [128, 768], BF16)
    t["cf"] = sb("cf", [128, 256], F32)
    t["idxa_s"] = sb("idxa_s", [128, 64], I32)
    t["idxb_s"] = sb("idxb_s", [128, 16], I32)
    t["brow"] = sb("brow", [1, 4096], BF16)
    t["smol"] = sb("smol", [128, 192], F32)
    t["nrow"] = sb("nrow", [1, 512], F32)
    t["junk"] = sb("junk", [128, 2048], F32)
    t["junk2"] = sb("junk2", [128, 2048], F32)
    rem = (nc.sbuf_bytes_remaining - 2048) & ~63
    assert rem >= 156 * 1024 + 1024, f"arena too small: {rem}"
    t["arena"] = sb("arena", [128, rem], U8)
    ps = lambda n, s, d: st.enter_context(nc.psum_tensor(n, s, d)).ap()
    t["PS0"] = ps("PS0", [128, 1024], F32)
    t["PS1"] = ps("PS1", [128, 1024], F32)
    t["PT0"] = ps("PT0", [128, 1024], BF16)
    t["PT1"] = ps("PT1", [128, 1024], BF16)
    t["PA0"] = ps("PA0", [128, 512], F32)
    t["PA1"] = ps("PA1", [128, 512], F32)
    return t


def ar(c, off, shape, dtype):
    esz = {F32: 4, BF16: 2, I32: 4}[dtype]
    v = c.t["arena"][:, off:off + shape[1] * esz].bitcast(dtype)
    if shape[0] != 128:
        v = v[0:shape[0]]
    return v


def smol(c, idx, w=1):
    return c.t["smol"][:, idx:idx + w]


def p_consts(c):
    t = c.t
    tk = [c.dma("sync", t["cbf"][:, :], t["cst_bf"]),
          c.dma("sync", t["cf"][:, :], t["cst_f"]),
          c.dma("sync", t["idxa_s"][:, :], t["idxa"]),
          c.dma("sync", t["idxb_s"][:, :], t["idxb"]),
          c.dma("gpsimd", t["brow"][0:1, 0:256], t["bq_h"].rearrange("a b -> (a b)").unsqueeze(0)),
          c.dma("gpsimd", t["brow"][0:1, 256:512], t["bkv_g"].rearrange("a b -> (a b)").unsqueeze(0)),
          c.dma("gpsimd", t["brow"][0:1, 512:1280], t["bqkv_h"].unsqueeze(0)),
          c.dma("gpsimd", t["brow"][0:1, 1280:3328], t["bproj"].unsqueeze(0))]
    e1 = c.dve(lambda e: e.memset(smol(c, 56), LN_EPS))
    e2 = c.dve(lambda e: e.memset(smol(c, 57), RMS_EPS))
    c.misc["tk_consts"] = tk + [e1, e2]


def p_ternary(c):
    t = c.t
    wbf = [ar(c, O_C + i * 4096, [128, 2048], BF16) for i in range(2)]
    absw = ar(c, O_C + 8192, [128, 2048], BF16)
    sgn = ar(c, O_C + 12288, [128, 2048], BF16)
    cond = ar(c, O_C + 16384, [128, 2048], BF16)
    qq = ar(c, O_C + 20480, [128, 2048], BF16)
    what = ar(c, O_C + 24576, [128, 2048], BF16)
    dd = ar(c, O_C + 28672, [128, 2048], BF16)
    weff = [ar(c, O_C + 32768 + i * 4096, [128, 2048], BF16) for i in range(2)]
    wts = [ar(c, O_C + 40960 + i * 4096, [128, 2048], BF16) for i in range(2)]
    gsum = smol(c, 160, 32)
    thr = smol(c, 128, 32)
    sclb = c.t["smol"][:, 64:80].bitcast(BF16)
    ident = t["cbf"][:, 0:128]
    g = {"tsg": [None, None], "lastT": [None, None], "dma": [None, None],
         "ev": [None, None]}
    prev_tc = prev_tq = prev_td = None
    gathers, gu_done, pr_done = [], [], []
    for u in range(24):
        pp = u % 2
        if u < 16:
            src = t["wgu_sh"][u * 128:(u + 1) * 128, :]
        else:
            rt, cc4 = (u - 16) % 2, (u - 16) // 2
            src = t["wpr_sh"][rt * 128:(rt + 1) * 128, cc4 * 2048:(cc4 + 1) * 2048]
        tl = c.dma("gpsimd", wbf[pp][:, :], src, wait=[g["tsg"][pp]])
        ta = c.act(lambda e, pp=pp: e.activation(absw[:, :], wbf[pp][:, :],
                                                 AF.Abs), wait=[tl, prev_tc])
        a3 = absw[:, :].rearrange("p (g k) -> p g k", k=64)
        tg = c.dve(lambda e: e.reduce_sum(gsum.unsqueeze(2), a3, AX), wait=[ta])
        tsc = c.dve(lambda e: e.tensor_scalar(sclb, gsum, 1.0 / 64, 1e-8,
                                              ALU.mult, ALU.max), wait=[tg])
        tth = c.dve(lambda e: e.tensor_scalar(thr, sclb, THR_C, None, ALU.mult),
                    wait=[tsc])
        th3 = thr.unsqueeze(2).broadcast_to([128, 32, 64])
        c3 = cond[:, :].rearrange("p (g k) -> p g k", k=64)
        tc = c.dve(lambda e: e.tensor_tensor(c3, a3, th3, ALU.is_gt), wait=[tth])
        tsg = c.act(lambda e, pp=pp: e.activation(sgn[:, :], wbf[pp][:, :], AF.Sign),
                    wait=[tl, prev_tq])
        g["tsg"][pp] = tsg
        tq = c.dve(lambda e: e.tensor_tensor(qq[:, :], sgn[:, :], cond[:, :],
                                             ALU.mult), wait=[tc, tsg])
        s3 = sclb.unsqueeze(2).broadcast_to([128, 32, 64])
        q3 = qq[:, :].rearrange("p (g k) -> p g k", k=64)
        w3h = what[:, :].rearrange("p (g k) -> p g k", k=64)
        twh = c.dve(lambda e: e.tensor_tensor(w3h, q3, s3, ALU.mult),
                    wait=[tq, prev_td])
        td = c.pool(lambda e, pp=pp: e.tensor_tensor(dd[:, :], what[:, :],
                                                     wbf[pp][:, :], ALU.subtract),
                    wait=[twh])
        tw = c.pool(lambda e, pp=pp: e.tensor_tensor(weff[pp][:, :], wbf[pp][:, :],
                                                     dd[:, :], ALU.add),
                    wait=[td, g["lastT"][pp]])
        prev_tc, prev_tq, prev_td = tc, tq, td
        ptb = (t["PT0"], t["PT1"])[pp]
        evs = []
        tT = None
        for grp in range(2):
            for k in range(8):
                kk = grp * 8 + k
                w_ = []
                if kk == 0:
                    w_ = [tw, g["ev"][pp]]
                if kk == 8:
                    w_ = [evs[0]]
                tT = c.tr(ptb[:, k * 128:(k + 1) * 128],
                          weff[pp][:, kk * 128:(kk + 1) * 128], ident,
                          wait=w_, inc=(k == 7))
            evs.append(c.dve(lambda e, grp=grp, pp=pp, ptb=ptb:
                             e.tensor_copy(wts[pp][:, grp * 1024:(grp + 1) * 1024],
                                           ptb[:, 0:1024]),
                             wait=[tT, g["dma"][pp]]))
        g["lastT"][pp] = tT
        g["ev"][pp] = evs[1]
        if u < 16:
            dst = t["wgu_c"].rearrange("(k p) i -> p k i", p=128)[:, :, u * 128:(u + 1) * 128]
        else:
            dst = t["wpr_c"][cc4 * 2048:(cc4 + 1) * 2048, rt * 128:(rt + 1) * 128] \
                .rearrange("(k p) i -> p k i", p=128)
        sdt = wts[pp][:, :].rearrange("p (k i) -> p k i", i=128)
        tdm = c.dma("sync", dst, sdt, wait=[evs[1]])
        g["dma"][pp] = tdm
        if u < 16:
            gu_done.append(tdm)
            if u == 15:
                gathers.append(c.gather(t["wgu_c"][:, :], t["g_wgu"][:, :],
                                        wait=gu_done))
        else:
            pr_done.append(tdm)
            if rt == 1:
                gathers.append(c.gather(t["wpr_c"][cc4 * 2048:(cc4 + 1) * 2048, :],
                                        t["g_wpr"][cc4 * 16384:(cc4 + 1) * 16384, :],
                                        wait=pr_done[-2:]))
    c.misc["w_cc"] = [gathers[0], gathers[-1]]
    c.misc["tern_pt"] = [g["ev"][0], g["ev"][1]]


def ln_tile(c, j, xt, xt_tok, xln_dst, resid_dst=None, out_dst=None):
    t = c.t
    pp = j % 2
    s1, s2 = smol(c, 0 + pp), smol(c, 2 + pp)
    m, nm2 = smol(c, 4 + pp), smol(c, 6 + pp)
    var, sd = smol(c, 8 + pp), smol(c, 10 + pp)
    rst, bln = smol(c, 12 + pp), smol(c, 14 + pp)
    junkA = ar(c, O_JK, [128, 2048], F32)
    junkB = ar(c, O_JK + 8192, [128, 2048], F32)
    xlb = ar(c, O_XLB + pp * 4096, [128, 2048], BF16)
    t1 = c.dve(lambda e: e.tensor_scalar(junkB[:, :], xt, 1.0, 0.0, ALU.mult,
                                         ALU.add, accum_out=s1), wait=xt_tok)
    t2 = c.act(lambda e: e.activation(junkA[:, :], xt, AF.Square, accum_out=s2),
               wait=xt_tok)
    tm = c.dve(lambda e: e.tensor_scalar(m, s1, 1.0 / E, None, ALU.mult), wait=[t1])
    tn = c.dve(lambda e: e.tensor_scalar(nm2, m, m, -1.0, ALU.mult, ALU.mult),
               wait=[tm])
    tv = c.dve(lambda e: e.scalar_tensor_tensor(var, s2, 1.0 / E, nm2,
                                                ALU.mult, ALU.add), wait=[t2, tn])
    ts_ = c.act(lambda e: e.activation(sd, var, AF.Sqrt, bias=smol(c, 56)),
                wait=[tv] + c.misc["tk_consts"][-2:])
    tr_ = c.dve(lambda e: e.reciprocal(rst, sd), wait=[ts_])
    tb = c.dve(lambda e: e.tensor_scalar(bln, m, rst, -1.0, ALU.mult, ALU.mult),
               wait=[tr_])
    ta = c.act(lambda e: e.activation(xlb[:, :], xt, AF.Identity, bias=bln,
                                      scale=rst), wait=[tb])
    tsl = c.touch("scalar", wait=[ta])
    td = c.dma("sync", xln_dst, xlb[:, :], wait=[tsl])
    trd = None
    if resid_dst is not None:
        trd = c.dma("sync", resid_dst, xt, wait=[ta])
    if out_dst is not None:
        trd = c.dma("sync", out_dst, xt, wait=[ta])
    return td, trd


def p_ln1(c):
    t = c.t
    colls, dmas = [], []
    xguard = [None, None]
    for j in range(NTT):
        pp = j % 2
        xt = ar(c, O_XT + pp * 8192, [128, 2048], F32)
        tx = c.dma("sync", xt[:, :], t["x_sh"][j * 128:(j + 1) * 128, :],
                   wait=[xguard[pp]])
        td, _ = ln_tile(c, j, xt[:, :], [tx], t["xln_c"][j * 128:(j + 1) * 128, :])
        dmas.append(td)
        xguard[pp] = td
        if j % 2 == 1:
            k = j // 2
            colls.append(c.gather(t["xln_c"][k * 256:(k + 1) * 256, :],
                                  t["g_xa"][k * 2048:(k + 1) * 2048, :],
                                  wait=dmas[-2:]))
    c.misc["xa_cc"] = colls


def p_resid1_ln2(c):
    t = c.t
    aa_last = c.misc["aa_cc"][-1]
    dm = []
    xguard = [None, None]
    abguard = [None, None]
    for j in range(NTT):
        pp = j % 2
        xt = ar(c, O_XT + pp * 8192, [128, 2048], F32)
        ab = ar(c, O_ATT + pp * 4096, [128, 2048], BF16)
        tx = c.dma("sync", xt[:, :], t["x_sh"][j * 128:(j + 1) * 128, :],
                   wait=[xguard[pp]])
        tg = []
        for r in range(8):
            tg.append(c.idma(ab[:, r * 256:(r + 1) * 256], t["g_aa"],
                             t["idxa_s"][:, j * 8 + r:j * 8 + r + 1],
                             wait=([aa_last, abguard[pp]] if r == 0 else [])))
        tadd = c.dve(lambda e: e.tensor_tensor(xt, xt, ab[:, :], ALU.add),
                     wait=[tx] + tg)
        abguard[pp] = tadd
        td, trd = ln_tile(c, j, xt[:, :], [tadd],
                          t["xln_loc"][j * 128:(j + 1) * 128, :],
                          resid_dst=t["resid"][j * 128:(j + 1) * 128, :])
        dm += [td, trd]
        xguard[pp] = trd
    c.misc["ln2_dma"] = dm


def p_attn_proj(c, which):
    t = c.t
    g_x = t["g_xa"] if which == "a" else t["g_xb"]
    cc = c.misc["xa_cc"] if which == "a" else c.misc["xb_cc"]
    ident = t["cbf"][:, 0:128]
    ones_row = t["cbf"][0:1, 256:768]
    if which == "a":
        ndout = 4
        ws = [ar(c, O_C + d * 4096, [128, 2048], BF16) for d in range(4)]
        rhs = [ar(c, O_C + 16384 + i * 8192, [128, 4096], BF16) for i in range(2)]
        qs0 = [ar(c, O_C + 32768 + i * 512, [128, 256], BF16) for i in range(2)]
        qs1 = [ar(c, O_C + 33792 + i * 512, [128, 256], BF16) for i in range(2)]
        vst = [ar(c, O_C + 34816 + i * 512, [128, 256], BF16) for i in range(2)]
        kg = ar(c, O_KV, [128, 8192], BF16)
        vg = ar(c, O_KV + 16384, [128, 8192], BF16)
        c.misc["kg"], c.misc["vg"] = kg, vg
        brows = [t["brow"][0:1, 0:128], t["brow"][0:1, 128:256],
                 t["brow"][0:1, 256:384], t["brow"][0:1, 384:512]]
        kdst = [kg]
        nv = 2
    else:
        ndout = 6
        vb = ar(c, O_C, [128, 16384], BF16)
        ws = [ar(c, O_C + 32768 + d * 4096, [128, 2048], BF16) for d in range(6)]
        rhs = [ar(c, O_C + 57344 + i * 8192, [128, 4096], BF16) for i in range(2)]
        qs0 = [ar(c, O_C + 73728 + i * 512, [128, 256], BF16) for i in range(2)]
        qs1 = [ar(c, O_C + 74752 + i * 512, [128, 256], BF16) for i in range(2)]
        vst = [ar(c, O_C + 75776 + i * 1024, [128, 512], BF16) for i in range(2)]
        kb = ar(c, O_KV, [128, 16384], BF16)
        c.misc["kb"], c.misc["vb"] = kb, vb
        brows = [t["brow"][0:1, 3328 + d * 128:3328 + (d + 1) * 128] for d in range(6)]
        nv = 4
    wtok = []
    if which == "a":
        srcs = [t["wq_h"][0], t["wq_h"][1], t["wk_g"], t["wv_g"]]
        for d in range(4):
            wtok.append(c.dma("gpsimd",
                              ws[d][:, :].rearrange("p (k i) -> p k i", i=128),
                              srcs[d].rearrange("(k p) i -> p k i", p=128)))
        bd = []
    else:
        raw = ar(c, O_C, [128, 12288], BF16)
        traw = c.dma("gpsimd", raw[:, :].rearrange("p (k i) -> p k i", i=768),
                     t["wqkv_h"].rearrange("(k p) i -> p k i", p=128))
        for d in range(6):
            sel, half = d // 2, d % 2
            tt = None
            for k in range(16):
                a0 = k * 768 + sel + 3 * half * 128
                src = raw[:, a0:a0 + 3 * 127 + 1:3]
                tt = c.dve(lambda e, d=d, k=k, src=src:
                           e.tensor_copy(ws[d][:, k * 128:(k + 1) * 128], src),
                           wait=[traw] if k == 0 else [])
            wtok.append(tt)
        bd = []
        for d in range(6):
            sel, half = d // 2, d % 2
            a0 = 512 + sel + 3 * half * 128
            src = t["brow"][0:1, a0:a0 + 3 * 127 + 1:3]
            bd.append(c.dve(lambda e, d=d, src=src:
                            e.tensor_copy(t["brow"][0:1, 3328 + d * 128:3328 + (d + 1) * 128],
                                          src),
                            wait=c.misc["tk_consts"][4:8]))
    PS = [t["PS0"][:, 0:512], t["PS0"][:, 512:1024],
          t["PS1"][:, 0:512], t["PS1"][:, 512:1024],
          t["PA0"][:, 0:512], t["PA1"][:, 0:512]]
    evac_tok = [None] * ndout
    mm_tok = [None] * ndout
    pt_guard = list(c.misc["tern_pt"]) if which == "a" else [None, None]
    qdma = []
    last_mm_prev = None
    for tb in range(32):
        pp = tb % 2
        kq, r = tb % 4, tb // 4
        rowb = kq * 2048 + r * 256
        tt = None
        for k in range(16):
            tt = c.dmat("scalar", rhs[pp][:, k * 256:(k + 1) * 256],
                        g_x[rowb:rowb + 256, k * 128:(k + 1) * 128],
                        wait=([cc[kq], last_mm_prev] if k == 0 else []))
        rhs_tok = tt
        for d in range(ndout):
            for k in range(16):
                c.mm(PS[d][:, 0:256], ws[d][:, k * 128:(k + 1) * 128],
                     rhs[pp][:, k * 256:(k + 1) * 256],
                     start=(k == 0), stop=False,
                     wait=([rhs_tok, wtok[d], evac_tok[d]] if k == 0 else []))
            mm_tok[d] = c.mm(PS[d][:, 0:256], brows[d], ones_row[0:1, 0:256],
                             start=False, stop=True,
                             wait=(c.misc["tk_consts"][4:8] + bd) if tb == 0 else [],
                             inc=True)
        last_mm_prev = mm_tok[ndout - 1]
        for d in range(ndout):
            if d < 2:
                dst = (qs0, qs1)[d][pp][:, :]
            elif which == "a" and d == 2:
                dst = c.misc["kg"][:, tb * 256:(tb + 1) * 256]
            elif which == "b" and d in (2, 3):
                dst = c.misc["kb"][:, (d - 2) * 8192 + tb * 256:(d - 2) * 8192 + (tb + 1) * 256]
            elif which == "a":
                dst = vst[pp][:, :]
            else:
                dst = vst[pp][:, (d - 4) * 256:(d - 4) * 256 + 256]
            evac_tok[d] = c.act(lambda e, d=d, dst=dst:
                                e.activation(dst, PS[d][:, 0:256], AF.Copy),
                                wait=[mm_tok[d]])
        qdst = t["q_a"] if which == "a" else t["q_b"]
        t0 = c.dma("sync", qdst[0:128, tb * 256:(tb + 1) * 256], qs0[pp][:, :],
                   wait=[evac_tok[0]])
        t1 = c.dma("sync", qdst[128:256, tb * 256:(tb + 1) * 256], qs1[pp][:, :],
                   wait=[evac_tok[1]])
        qdma += [t0, t1]
        ptb = (t["PT0"], t["PT1"])[pp]
        tT = None
        if which == "a":
            order = [0, 128]
        else:
            order = [0, 256, 128, 384]
        for x, so in enumerate(order):
            w_ = []
            if x == 0:
                w_ = [evac_tok[ndout - 1], evac_tok[ndout - 2], pt_guard[pp]]
            tT = c.tr(ptb[:, x * 128:(x + 1) * 128],
                      vst[pp][:, so:so + 128],
                      ident, wait=w_, inc=(x == nv - 1))
        if which == "a":
            pt_guard[pp] = c.dve(lambda e, tb=tb, ptb=ptb:
                                 e.tensor_copy(c.misc["vg"][:, tb * 256:(tb + 1) * 256],
                                               ptb[:, 0:256]),
                                 wait=[tT])
        else:
            pt_guard[pp] = c.dve(lambda e, tb=tb, ptb=ptb:
                                 e.tensor_copy(c.misc["vb"][:, tb * 512:(tb + 1) * 512],
                                               ptb[:, 0:512]),
                                 wait=[tT])
    c.misc["proj_done_" + which] = [last_mm_prev, pt_guard[0], pt_guard[1]] \
        + list(evac_tok) + qdma[-2:]


def _norm_prep(c, srcs, out_slot, prep_wait):
    t = c.t
    ones_c = t["cbf"][:, 256:257]
    ksq = ar(c, O_C + 36864, [128, 1024], BF16)
    nr = t["nrow"][0:1, 0:512]
    res = smol(c, out_slot)[0:1, :]
    tmp = smol(c, 30)[0:1, :]
    first = True
    tev = None
    tmm = None
    tsq = None
    for get in srcs:
        buf, tok = get([tsq] if tsq else [])
        tsq = c.act(lambda e, buf=buf: e.activation(ksq[:, :], buf, AF.Square),
                    wait=[tok] + (prep_wait if first else []) + ([tmm] if tmm else []))
        for half in range(2):
            tmm = c.mm(t["PA0"][0:1, 0:512], ones_c, ksq[:, half * 512:(half + 1) * 512],
                       start=True, stop=True,
                       wait=[tsq] + ([tev] if tev else []), inc=True)
            tev = c.act(lambda e: e.activation(nr, t["PA0"][0:1, 0:512], AF.Copy),
                        wait=[tmm])
            tevf = c.touch("scalar", wait=[tev])
            trm = c.dve(lambda e: e.reduce_max(tmp, nr, AX), wait=[tevf])
            if first:
                tacc = c.dve(lambda e: e.tensor_copy(res, tmp), wait=[trm])
                first = False
            else:
                tacc = c.dve(lambda e: e.tensor_tensor(res, res, tmp, ALU.max),
                             wait=[trm])
    return tacc


def p_attn_a(c):
    t = c.t
    kg, vg = c.misc["kg"], c.misc["vg"]
    ident = t["cbf"][:, 0:128]
    tri = t["cf"][:, 128:256]
    prep_w = c.misc["proj_done_a"]
    qbuf_n = ar(c, O_C + 38912, [128, 1024], BF16)
    tk2 = _norm_prep(c, [
        (lambda w, ch=ch: (kg[:, ch * 1024:(ch + 1) * 1024], None))
        for ch in range(8)], 31, prep_w)
    q2 = []
    for h in range(2):
        def mk(ch, h=h):
            def get(w):
                tql = c.dma("sync", qbuf_n[:, :],
                            t["q_a"][h * 128:(h + 1) * 128, ch * 1024:(ch + 1) * 1024],
                            wait=w + prep_w)
                return qbuf_n[:, :], tql
            return get
        q2.append(_norm_prep(c, [mk(ch) for ch in range(8)], 32 + h, []))
    k2r = smol(c, 31)[0:1, :]
    negb = [smol(c, 36 + h) for h in range(2)]
    bs = []
    for h in range(2):
        q2r = smol(c, 32 + h)[0:1, :]
        sq1 = smol(c, 168 + 2 * h)[0:1, :]
        sq2 = smol(c, 169 + 2 * h)[0:1, :]
        t1 = c.act(lambda e: e.activation(sq1, k2r, AF.Sqrt), wait=[tk2, q2[1]])
        t2 = c.act(lambda e: e.activation(sq2, q2r, AF.Sqrt), wait=[t1])
        tf = c.touch("scalar", wait=[t2])
        tf = c.touch("scalar", wait=[tf])
        t3 = c.dve(lambda e: e.tensor_tensor(sq1, sq1, sq2, ALU.mult), wait=[tf])
        t4 = c.dve(lambda e: e.tensor_scalar(sq2, sq1, -SCALE_A, None, ALU.mult),
                   wait=[t3])
        t5 = c.touch("vector", wait=[t4])
        t5 = c.touch("vector", wait=[t5])
        t6 = c.dma("sync", t["bounce"][0:1, h:h + 1], sq2, wait=[t5])
        t7 = c.dma("sync", negb[h], t["bounce"][0:1, h:h + 1].broadcast_to([128, 1]),
                   wait=[t6])
        bs.append(t7)
    steps = []
    for tq in range(64):
        nsc = tq // 8 + 1
        for h in range(2):
            for sc in range(nsc):
                L = min(1024, (tq + 1) * 128 - sc * 1024)
                steps.append((tq, h, sc, nsc, L))
    n = len(steps)
    qb = [ar(c, O_C + 40960 + i * 256, [128, 128], BF16) for i in range(2)]
    Pb = [ar(c, O_C + 41984 + i * 2048, [128, 1024], BF16) for i in range(2)]
    PTs = [ar(c, O_C + 46080 + i * 2048, [128, 1024], BF16) for i in range(2)]
    ob = [ar(c, O_C + 50176 + i * 512, [128, 256], BF16) for i in range(2)]
    SB = [t["PS0"], t["PS1"]]
    PTB = [t["PT0"], t["PT1"]]
    AVB = [t["PA0"], t["PA1"]]
    sa = [smol(c, 64, 8), smol(c, 72, 8)]
    sum_fin = [smol(c, 40), smol(c, 41)]
    rfin = [smol(c, 44), smol(c, 45)]
    s_done = [None] * n
    exp_done = [None] * n
    t_done = [None] * n
    ev_done = [None] * n
    av_done = [None] * n
    mask_done = [None] * n
    av_free = [bs[1], bs[1]]
    obuf_free = [None, None]
    fin_guard = [None, None]
    ptg = [prep_w[1], prep_w[2]]
    q_ld = {}
    units = []
    ui = -1
    for i, (tq, h, sc, nsc, L) in enumerate(steps):
        if sc == 0:
            ui += 1
            units.append([ui, tq, h, i, i + nsc - 1])
    step_unit = {}
    for u in units:
        for i in range(u[3], u[4] + 1):
            step_unit[i] = u
    aa_store = {}
    aa_cc = []
    for i in range(n + 3):
        k3 = i - 3
        if 0 <= k3 < n:
            tq, h, sc, nsc, L = steps[k3]
            u = step_unit[k3]
            uu = u[0]
            tAV = None
            for kk in range(L // 128):
                w_ = []
                if kk == 0:
                    w_ = [ev_done[k3]]
                    if sc == 0:
                        w_.append(av_free[uu % 2])
                tAV = c.mm(AVB[uu % 2][:, 0:128],
                           PTs[k3 % 2][:, kk * 128:(kk + 1) * 128],
                           vg[:, (sc * 8 + kk) * 128:(sc * 8 + kk + 1) * 128],
                           start=(sc == 0 and kk == 0),
                           stop=(k3 == u[4] and kk == L // 128 - 1),
                           wait=w_, inc=(kk == L // 128 - 1))
            av_done[k3] = tAV
            if k3 == u[4]:
                tred = c.dve(lambda e, uu=uu, nsc=nsc:
                             e.reduce_sum(sum_fin[uu % 2], sa[uu % 2][:, 0:nsc], AX),
                             wait=[exp_done[k3]])
                fin_guard[uu % 2] = tred
                trc = c.dve(lambda e, uu=uu: e.reciprocal(rfin[uu % 2],
                                                          sum_fin[uu % 2]),
                            wait=[tred, av_free[uu % 2]])
                tf = c.touch("vector", wait=[trc])
                tev = c.act(lambda e, uu=uu:
                            e.activation(ob[uu % 2][:, 0:128], AVB[uu % 2][:, 0:128],
                                         AF.Copy, scale=rfin[uu % 2]),
                            wait=[tAV, tf, obuf_free[uu % 2]])
                av_free[uu % 2] = tev
                tsl = c.touch("scalar", wait=[tev])
                tst = c.dma("sync",
                            t["aa_c"][tq * 128:(tq + 1) * 128, h * 128:(h + 1) * 128],
                            ob[uu % 2][:, 0:128], wait=[tsl])
                obuf_free[uu % 2] = tst
                aa_store.setdefault(tq // 8, []).append(tst)
                if tq % 8 == 7 and h == 1:
                    kb_ = tq // 8
                    aa_cc.append(c.gather(
                        t["aa_c"][kb_ * 1024:(kb_ + 1) * 1024, :],
                        t["g_aa"][kb_ * 8192:(kb_ + 1) * 8192, :],
                        wait=aa_store[kb_]))
        j2 = i - 2
        if 0 <= j2 < n:
            tq, h, sc, nsc, L = steps[j2]
            u = step_unit[j2]
            uu = u[0]
            exp_done[j2] = c.act(
                lambda e, j2=j2, L=L, h=h, sc=sc, uu=uu:
                e.activation(Pb[j2 % 2][:, 0:L], SB[j2 % 2][:, 0:L], AF.Exp,
                             bias=negb[h], scale=SCALE_A,
                             accum_out=sa[uu % 2][:, sc:sc + 1]),
                wait=[mask_done[j2] or s_done[j2], bs[h],
                      t_done[j2 - 2] if j2 >= 2 else None,
                      fin_guard[uu % 2] if sc == 0 else None])
        if i < n:
            tq, h, sc, nsc, L = steps[i]
            u = step_unit[i]
            if sc == 0:
                uu = u[0]
                if uu == 0:
                    q_ld[0] = c.dma("sync", qb[0][:, :],
                                    t["q_a"][0:128, 0:128], wait=prep_w)
                if uu + 1 < len(units):
                    nu = units[uu + 1]
                    guard = s_done[units[uu - 1][4]] if uu >= 1 else None
                    q_ld[uu + 1] = c.dma(
                        "sync", qb[(uu + 1) % 2][:, :],
                        t["q_a"][nu[2] * 128:(nu[2] + 1) * 128,
                                 nu[1] * 128:(nu[1] + 1) * 128],
                        wait=[guard])
            tS = None
            for half in range((L + 511) // 512):
                w = min(512, L - half * 512)
                w_ = [q_ld[u[0]], exp_done[i - 2] if i >= 2 else None] \
                    if half == 0 else []
                tS = c.mm(SB[i % 2][:, half * 512:half * 512 + w],
                          qb[u[0] % 2][:, :],
                          kg[:, sc * 1024 + half * 512: sc * 1024 + half * 512 + w],
                          start=True, stop=True,
                          wait=w_, inc=(half == (L - 1) // 512))
            s_done[i] = tS
        j = i - 1
        if 0 <= j < n:
            tq, h, sc, nsc, L = steps[j]
            if sc == nsc - 1:
                mask_done[j] = c.dve(lambda e, j=j, L=L:
                                     e.tensor_tensor(SB[j % 2][:, L - 128:L],
                                                     SB[j % 2][:, L - 128:L],
                                                     tri, ALU.add),
                                     wait=[s_done[j]])
        if 0 <= j2 < n:
            tq, h, sc, nsc, L = steps[j2]
            tT = None
            for kk in range(L // 128):
                w_ = [exp_done[j2], ptg[j2 % 2]] if kk == 0 else []
                tT = c.tr(PTB[j2 % 2][:, kk * 128:(kk + 1) * 128],
                          Pb[j2 % 2][:, kk * 128:(kk + 1) * 128], ident,
                          wait=w_, inc=(kk == L // 128 - 1))
            t_done[j2] = tT
            ev_done[j2] = c.dve(lambda e, j2=j2, L=L:
                                e.tensor_copy(PTs[j2 % 2][:, 0:L], PTB[j2 % 2][:, 0:L]),
                                wait=[tT, av_done[j2 - 2] if j2 >= 2 else None])
            ptg[j2 % 2] = ev_done[j2]
    c.misc["aa_cc"] = aa_cc


def p_attn_b(c):
    t = c.t
    kb, vb = c.misc["kb"], c.misc["vb"]
    ident = t["cbf"][:, 0:128]
    prep_w = c.misc["proj_done_b"]
    n = 64 * 8
    qb = [ar(c, O_C + 32768 + i * 512, [128, 256], BF16) for i in range(2)]
    Pb = [ar(c, O_C + 34816 + i * 2048, [128, 1024], BF16) for i in range(2)]
    PTs = [ar(c, O_C + 38912 + i * 2048, [128, 1024], BF16) for i in range(2)]
    avst = [ar(c, O_C + 43008 + i * 512, [128, 256], BF16) for i in range(2)]
    avT = [ar(c, O_C + 44032 + i * 512, [128, 256], BF16) for i in range(2)]
    SB = [t["PS0"], t["PS1"]]
    PTB = [t["PT0"], t["PT1"]]
    AVB = [t["PA0"], t["PA1"]]
    nm = [smol(c, 46), smol(c, 47)]
    sa = [smol(c, 64, 8), smol(c, 72, 8)]
    rn = [smol(c, 80, 8), smol(c, 88, 8)]
    darr = smol(c, 96, 8)
    earr = smol(c, 104, 8)
    negm_sc = smol(c, 52)
    dn = smol(c, 53)
    alpha = smol(c, 54)
    sum_fin = [smol(c, 40), smol(c, 41)]
    rfin = [smol(c, 44), smol(c, 45)]
    sfin = [smol(c, 58), smol(c, 59)]
    s_done = [None] * n
    exp_done = [None] * n
    t_done = [None] * n
    ev_done = [None] * n
    av_done = [None] * n
    stat_done = [None] * n
    alpha_done = [None] * n
    resc = [None] * n
    av_free = [prep_w[1], prep_w[2]]
    obuf_free = [None, None]
    fin_guard = [None, None]
    ptg = [prep_w[1], prep_w[2]]
    q_ld = {}
    ab_store = {}
    ab_cc = []
    for i in range(n + 3):
        k3 = i - 3
        if 0 <= k3 < n:
            tq, sc = k3 // 8, k3 % 8
            uu = tq
            tAV = None
            for kk in range(8):
                w_ = []
                if kk == 0:
                    w_ = [ev_done[k3], resc[k3]]
                    if sc == 0:
                        w_.append(av_free[uu % 2])
                tAV = c.mm(AVB[uu % 2][:, 0:256],
                           PTs[k3 % 2][:, kk * 128:(kk + 1) * 128],
                           vb[:, (sc * 8 + kk) * 256:(sc * 8 + kk + 1) * 256],
                           start=(sc == 0 and kk == 0),
                           stop=(sc == 7 and kk == 7),
                           wait=w_, inc=(kk == 7))
            av_done[k3] = tAV
            if sc == 7:
                td_ = c.dve(lambda e, uu=uu:
                            e.tensor_scalar(darr, rn[uu % 2],
                                            rn[uu % 2][:, 7:8], -1.0,
                                            ALU.subtract, ALU.mult),
                            wait=[stat_done[k3]])
                tf = c.touch("vector", wait=[td_])
                te_ = c.act(lambda e: e.activation(earr, darr, AF.Exp), wait=[tf])
                tf2 = c.touch("scalar", wait=[te_])
                tp_ = c.dve(lambda e, uu=uu:
                            e.tensor_tensor(darr, sa[uu % 2], earr, ALU.mult),
                            wait=[tf2, exp_done[k3]])
                fin_guard[uu % 2] = tp_
                tred = c.dve(lambda e, uu=uu:
                             e.reduce_sum(sum_fin[uu % 2], darr, AX),
                             wait=[tp_, av_free[uu % 2]])
                trc = c.dve(lambda e, uu=uu:
                            e.reciprocal(rfin[uu % 2], sum_fin[uu % 2]), wait=[tred])
                tsf = c.dve(lambda e, uu=uu:
                            e.tensor_scalar(sfin[uu % 2], rfin[uu % 2], INV_SQ_E,
                                            None, ALU.mult), wait=[trc])
                tf3 = c.touch("vector", wait=[tsf])
                tev = c.act(lambda e, uu=uu:
                            e.activation(avst[uu % 2][:, :], AVB[uu % 2][:, 0:256],
                                         AF.Copy, scale=sfin[uu % 2]),
                            wait=[tAV, tf3, obuf_free[uu % 2]])
                av_free[uu % 2] = tev
                tT2 = None
                for x in range(2):
                    w_ = [tev, ptg[k3 % 2]] if x == 0 else []
                    tT2 = c.tr(PTB[k3 % 2][:, x * 128:(x + 1) * 128],
                               avst[uu % 2][:, x * 128:(x + 1) * 128], ident,
                               wait=w_, inc=(x == 1))
                tevT = c.dve(lambda e, uu=uu, k3=k3:
                             e.tensor_copy(avT[uu % 2][:, :], PTB[k3 % 2][:, 0:256]),
                             wait=[tT2])
                ptg[k3 % 2] = tevT
                tsl = c.touch("vector", wait=[tevT])
                tb_ = tq // 8
                qc = (tq % 8) * 128
                dst = t["ab_c"][tb_ * 256:(tb_ + 1) * 256, qc:qc + 128] \
                    .rearrange("(x p) i -> p x i", p=128)
                tst = c.dma("sync", dst,
                            avT[uu % 2][:, :].rearrange("p (x i) -> p x i", x=2),
                            wait=[tsl])
                obuf_free[uu % 2] = tst
                ab_store.setdefault(tb_, []).append(tst)
                if tq % 8 == 7:
                    ab_cc.append(c.gather(
                        t["ab_c"][tb_ * 256:(tb_ + 1) * 256, :],
                        t["g_ab"][tb_ * 2048:(tb_ + 1) * 2048, :],
                        wait=ab_store[tb_]))
        j2 = i - 2
        if 0 <= j2 < n:
            tq, sc = j2 // 8, j2 % 8
            uu = tq
            if sc > 0:
                alpha_done[j2] = c.act(lambda e: e.activation(alpha, dn, AF.Exp),
                                       wait=[stat_done[j2]])
                resc[j2] = c.act(lambda e, uu=uu:
                                 e.activation(AVB[uu % 2][:, 0:256],
                                              AVB[uu % 2][:, 0:256],
                                              AF.Copy, scale=alpha),
                                 wait=[alpha_done[j2], av_done[j2 - 1]])
            exp_done[j2] = c.act(
                lambda e, j2=j2, sc=sc, uu=uu:
                e.activation(Pb[j2 % 2][:, 0:1024], SB[j2 % 2][:, 0:1024], AF.Exp,
                             bias=nm[sc % 2], scale=1.0,
                             accum_out=sa[uu % 2][:, sc:sc + 1]),
                wait=[stat_done[j2],
                      t_done[j2 - 2] if j2 >= 2 else None,
                      fin_guard[uu % 2] if sc == 0 else None])
        if i < n:
            tq, sc = i // 8, i % 8
            uu = tq
            if sc == 0:
                if uu == 0:
                    q_ld[0] = c.dma("sync",
                                    qb[0][:, :].rearrange("p (a i) -> p a i", i=128),
                                    t["q_b"][:, 0:128].rearrange("(a p) i -> p a i", p=128),
                                    wait=prep_w)
                if uu + 1 < 64:
                    guard = s_done[(uu - 1) * 8 + 7] if uu >= 1 else None
                    q_ld[uu + 1] = c.dma(
                        "sync",
                        qb[(uu + 1) % 2][:, :].rearrange("p (a i) -> p a i", i=128),
                        t["q_b"][:, (uu + 1) * 128:(uu + 2) * 128]
                        .rearrange("(a p) i -> p a i", p=128),
                        wait=[guard])
            tS = None
            for half in range(2):
                for dch in range(2):
                    w_ = [q_ld[uu], exp_done[i - 2] if i >= 2 else None] \
                        if (half == 0 and dch == 0) else []
                    tS = c.mm(SB[i % 2][:, half * 512:(half + 1) * 512],
                              qb[uu % 2][:, dch * 128:(dch + 1) * 128],
                              kb[:, dch * 8192 + sc * 1024 + half * 512:
                                 dch * 8192 + sc * 1024 + (half + 1) * 512],
                              start=(dch == 0), stop=(dch == 1),
                              wait=w_, inc=(half == 1 and dch == 1))
            s_done[i] = tS
        j = i - 1
        if 0 <= j < n:
            tq, sc = j // 8, j % 8
            uu = tq
            tmx = c.dve(lambda e, j=j:
                        e.reduce_max(negm_sc, SB[j % 2][:, 0:1024:4], AX, negate=True),
                        wait=[s_done[j]])
            if sc == 0:
                tnm = c.dve(lambda e: e.tensor_copy(nm[0], negm_sc), wait=[tmx])
            else:
                tnm = c.dve(lambda e, sc=sc:
                            e.tensor_tensor(nm[sc % 2], nm[(sc - 1) % 2], negm_sc,
                                            ALU.min), wait=[tmx])
                tnm = c.dve(lambda e, sc=sc:
                            e.tensor_tensor(dn, nm[sc % 2], nm[(sc - 1) % 2],
                                            ALU.subtract), wait=[tnm])
            trn_ = c.dve(lambda e, uu=uu, sc=sc:
                         e.tensor_copy(rn[uu % 2][:, sc:sc + 1], nm[sc % 2]),
                         wait=[tnm, fin_guard[uu % 2] if sc == 0 else None])
            stat_done[j] = trn_
        if 0 <= j2 < n:
            tT = None
            for kk in range(8):
                w_ = [exp_done[j2], ptg[j2 % 2]] if kk == 0 else []
                tT = c.tr(PTB[j2 % 2][:, kk * 128:(kk + 1) * 128],
                          Pb[j2 % 2][:, kk * 128:(kk + 1) * 128], ident,
                          wait=w_, inc=(kk == 7))
            t_done[j2] = tT
            ev_done[j2] = c.dve(lambda e, j2=j2:
                                e.tensor_copy(PTs[j2 % 2][:, 0:1024], PTB[j2 % 2][:, 0:1024]),
                                wait=[tT, av_done[j2 - 2] if j2 >= 2 else None])
            ptg[j2 % 2] = ev_done[j2]
    c.misc["ab_cc"] = ab_cc


def p_ffn(c, p):
    t = c.t
    ident = t["cbf"][:, 0:128]
    ones_c = t["cbf"][:, 256:257]
    wcc = c.misc["w_cc"]
    lnw = c.misc["ln2_dma"] if p == 1 else c.misc["ln4_dma"]
    wg = [ar(c, O_C + i * 4096, [128, 2048], BF16) for i in range(2)]
    wu = [ar(c, O_C + 8192 + i * 4096, [128, 2048], BF16) for i in range(2)]
    gs = ar(c, O_C + 16384, [128, 512], F32)
    us = ar(c, O_C + 18432, [128, 512], F32)
    hf = ar(c, O_C + 20480, [128, 512], F32)
    sqb = ar(c, O_C + 22528, [128, 512], BF16)
    hbf = [ar(c, O_C + 23552 + i * 1024, [128, 512], BF16) for i in range(2)]
    s2row = ar(c, O_C + 26624, [1, 1024], F32)
    rstd8 = ar(c, O_C + 30720, [8, 128], F32)
    rstdc = ar(c, O_C + 31232, [128, 8], F32)
    wp = [ar(c, O_C + 32768 + i * 1024, [128, 512], BF16) for i in range(2)]
    hb = [ar(c, O_C + 34816 + i * 2048, [128, 1024], BF16) for i in range(2)]
    ffo = ar(c, O_C + 40960, [128, 16384], BF16)
    xfm = ar(c, O_KV, [128, 16384], BF16)
    txf = None
    for k in range(16):
        txf = c.dmat("scalar", xfm[:, k * 1024:(k + 1) * 1024],
                      t["xln_loc"][0:1024, k * 128:(k + 1) * 128],
                      wait=lnw if k == 0 else [])
    banksG = [t["PS0"][:, 0:512], t["PS1"][:, 0:512]]
    banksU = [t["PS0"][:, 512:1024], t["PS1"][:, 512:1024]]
    banksS = [t["PA0"][0:1, 0:512], t["PA1"][0:1, 0:512]]
    ev_free = [None, None]
    wfree = [None, None]
    hdma = []
    tones = [None, None]
    for ht in range(64):
        pp = ht % 2
        rg, cg = ht // 16, (ht * 128) % 2048
        twg = c.dma("gpsimd",
                    wg[pp][:, :].rearrange("p (k i) -> p k i", i=128),
                    t["g_wgu"][rg * 2048:(rg + 1) * 2048, cg:cg + 128]
                    .rearrange("(k p) i -> p k i", p=128),
                    wait=[wcc[0], wfree[pp]])
        twu = c.dma("gpsimd",
                    wu[pp][:, :].rearrange("p (k i) -> p k i", i=128),
                    t["g_wgu"][(4 + rg) * 2048:(5 + rg) * 2048, cg:cg + 128]
                    .rearrange("(k p) i -> p k i", p=128),
                    wait=[])
        for th in range(2):
            step = (ht * 2 + th) % 2
            tmG = tmU = None
            for k in range(16):
                w_ = [twg, twu, ev_free[step], txf] if k == 0 else []
                tmG = c.mm(banksG[step], wg[pp][:, k * 128:(k + 1) * 128],
                           xfm[:, k * 1024 + th * 512:k * 1024 + (th + 1) * 512],
                           start=(k == 0), stop=(k == 15), wait=w_,
                           inc=(k == 15))
                tmU = c.mm(banksU[step], wu[pp][:, k * 128:(k + 1) * 128],
                           xfm[:, k * 1024 + th * 512:k * 1024 + (th + 1) * 512],
                           start=(k == 0), stop=(k == 15), wait=[],
                           inc=(k == 15))
            tsil = c.act(lambda e, step=step: e.activation(gs[:, :], banksG[step],
                                                           AF.Silu), wait=[tmG])
            tup = c.act(lambda e, step=step: e.activation(us[:, :], banksU[step],
                                                          AF.Copy), wait=[tmU])
            thf = c.dve(lambda e: e.tensor_tensor(hf[:, :], gs[:, :], us[:, :],
                                                  ALU.mult), wait=[tsil, tup])
            ev_free[step] = thf
            tsq = c.act(lambda e: e.activation(sqb[:, :], hf[:, :], AF.Square),
                        wait=[thf])
            tones[th] = c.mm(banksS[th], ones_c, sqb[:, :],
                             start=(ht == 0), stop=(ht == 63),
                             wait=[tsq] + ([wcc[0]] if ht == 0 else []), inc=True)
            thb = c.dve(lambda e, pp2=(ht * 2 + th) % 2:
                        e.tensor_copy(hbf[pp2][:, :], hf[:, :]), wait=[tsq])
            tdh = c.dma("sync", t["h_d"][ht * 128:(ht + 1) * 128,
                                         th * 512:(th + 1) * 512],
                        hbf[(ht * 2 + th) % 2][:, :], wait=[thb])
            hdma.append(tdh)
        wfree[pp] = tmU
    te0 = c.act(lambda e: e.activation(s2row[0:1, 0:512], banksS[0], AF.Copy),
                wait=[tones[0]])
    te1 = c.act(lambda e: e.activation(s2row[0:1, 512:1024], banksS[1], AF.Copy),
                wait=[tones[1]])
    tsd = c.act(lambda e: e.activation(s2row[0:1, :], s2row[0:1, :], AF.Sqrt,
                                       scale=1.0 / HFF, bias=smol(c, 57)[0:1, :]),
                wait=[te0, te1] + c.misc["tk_consts"][-2:])
    trc = c.dve(lambda e: e.reciprocal(s2row[0:1, :], s2row[0:1, :]), wait=[tsd])
    tt_ = c.touch("vector", wait=[trc])
    tbo = c.dma("sync", t["bounce"][0:1, 0:1024], s2row[0:1, :], wait=[tt_])
    tb8 = c.dma("sync", rstd8[0:8, :],
                t["bounce"][0:1, 0:1024].rearrange("o (a b) -> (o a) b", a=8),
                wait=[tbo])
    tT8 = c.tr(t["PA0"][:, 0:8], rstd8[0:8, :], t["cf"][0:8, 0:8],
               wait=[tb8, tones[0], te0, te1], inc=True)
    trsc = c.dve(lambda e: e.tensor_copy(rstdc[:, :], t["PA0"][:, 0:8]), wait=[tT8])
    PT0f = t["PT0"].bitcast(F32)
    PT1f = t["PT1"].bitcast(F32)
    banks8 = [t["PS0"][:, 0:512], t["PS0"][:, 512:1024],
              t["PS1"][:, 0:512], t["PS1"][:, 512:1024],
              t["PA0"][:, 0:512], t["PA1"][:, 0:512],
              PT0f[:, 0:512], PT1f[:, 0:512]]
    ev8 = [None] * 8
    last_pm = None
    for dq in range(4):
        for hc in range(64):
            pp = hc % 2
            cc4, rr = hc // 16, (hc % 16) * 128
            w1 = [last_pm]
            tw1 = c.dma("gpsimd", wp[pp][:, 0:256],
                        t["g_wpr"][cc4 * 16384 + (2 * dq) * 2048 + rr:
                                   cc4 * 16384 + (2 * dq) * 2048 + rr + 128, :],
                        wait=[wcc[1]] + w1)
            tw2 = c.dma("gpsimd", wp[pp][:, 256:512],
                        t["g_wpr"][cc4 * 16384 + (2 * dq + 1) * 2048 + rr:
                                   cc4 * 16384 + (2 * dq + 1) * 2048 + rr + 128, :],
                        wait=[])
            th_ = c.dma("sync", hb[pp][:, :], t["h_d"][hc * 128:(hc + 1) * 128, :],
                        wait=[hdma[2 * hc + 1] if dq == 0 else hdma[-1], last_pm])
            for j4 in range(4):
                for th in range(2):
                    w_ = []
                    if j4 == 0 and th == 0:
                        w_ = [tw1, tw2, th_]
                        if hc == 0:
                            w_ += [ev8[x] for x in range(8)]
                            w_ += [tT8, trsc]
                    last_pm = c.mm(banks8[j4 * 2 + th],
                                   wp[pp][:, j4 * 128:(j4 + 1) * 128],
                                   hb[pp][:, th * 512:(th + 1) * 512],
                                   start=(hc == 0), stop=(hc == 63),
                                   wait=w_, inc=(j4 == 3 and th == 1))
        for j4 in range(4):
            for th in range(2):
                ev8[j4 * 2 + th] = c.act(
                    lambda e, dq=dq, j4=j4, th=th:
                    e.activation(ffo[:, (dq * 4 + j4) * 1024 + th * 512:
                                      (dq * 4 + j4) * 1024 + (th + 1) * 512],
                                 banks8[j4 * 2 + th], AF.Copy),
                    wait=[last_pm])
    colls, dmas = [], []
    xguard = [None, None]
    ptg = [ev8[6], ev8[7]]
    for j in range(NTT):
        pp = j % 2
        xt = ar(c, O_XT + pp * 8192, [128, 2048], F32)
        tx = c.dma("sync", xt[:, :], t["resid"][j * 128:(j + 1) * 128, :],
                   wait=[xguard[pp]])
        stt_toks = []
        for grp in range(2):
            tT = None
            for kk in range(8):
                dt = grp * 8 + kk
                w_ = []
                if kk == 0:
                    w_ = [ev8[7], ptg[pp]] + (stt_toks[-1:] if grp == 1 else [])
                tT = c.tr(t["PT0" if pp == 0 else "PT1"][:, kk * 128:(kk + 1) * 128],
                          ffo[:, dt * 1024 + j * 128:dt * 1024 + (j + 1) * 128],
                          ident, wait=w_, inc=(kk == 7))
            for kk in range(8):
                dt = grp * 8 + kk
                stt_toks.append(c.dve(
                    lambda e, dt=dt, kk=kk, pp=pp:
                    e.scalar_tensor_tensor(xt[:, dt * 128:(dt + 1) * 128],
                                           c.t["PT0" if pp == 0 else "PT1"][:, kk * 128:(kk + 1) * 128],
                                           rstdc[:, j:j + 1],
                                           xt[:, dt * 128:(dt + 1) * 128],
                                           ALU.mult, ALU.add),
                    wait=[tT, tx, trsc] if kk == 0 else []))
        ptg[pp] = stt_toks[-1]
        if p == 1:
            td, trd = ln_tile(c, j, xt[:, :], stt_toks[-8:],
                              t["xln_c"][j * 128:(j + 1) * 128, :],
                              resid_dst=t["resid"][j * 128:(j + 1) * 128, :])
            dmas.append(td)
            xguard[pp] = trd
            if j % 2 == 1:
                k = j // 2
                colls.append(c.gather(t["xln_c"][k * 256:(k + 1) * 256, :],
                                      t["g_xb"][k * 2048:(k + 1) * 2048, :],
                                      wait=dmas[-2:]))
        else:
            tsl = c.touch("vector", wait=stt_toks[-8:])
            tso = c.dma("sync", t["out_sh"][j * 128:(j + 1) * 128, :], xt[:, :],
                        wait=[tsl] + stt_toks[-8:])
            xguard[pp] = tso
            dmas.append(tso)
    if p == 1:
        c.misc["xb_cc"] = colls
    else:
        c.misc["out_dma"] = dmas


def p_oproj(c):
    t = c.t
    ident = t["cbf"][:, 0:128]
    ones_row = t["cbf"][0:1, 256:768]
    ab_last = c.misc["ab_cc"][-1]
    avb = ar(c, O_C, [128, 16384], BF16)
    oout = ar(c, O_C + 32768, [128, 16384], BF16)
    wpo = [ar(c, O_C + 65536 + i * 4096, [128, 2048], BF16) for i in range(2)]
    tgv = []
    for k in range(16):
        tgv.append(c.idma(avb[:, k * 1024:(k + 1) * 1024], t["g_ab"],
                          t["idxb_s"][:, k:k + 1],
                          wait=[ab_last] if k == 0 else []))
    PSb = [t["PS0"], t["PS1"]]
    ev_tok = [None, None]
    mm_last = None
    for ot in range(16):
        pp = ot % 2
        twp = c.dma("gpsimd",
                    wpo[pp][:, :].rearrange("p (k i) -> p k i", i=128),
                    t["wproj"][:, ot * 128:(ot + 1) * 128]
                    .rearrange("(k p) i -> p k i", p=128),
                    wait=[mm_last] if ot >= 2 else [])
        tmm = None
        for half in range(2):
            for k in range(16):
                w_ = []
                if k == 0:
                    w_ = [twp, ev_tok[pp]] + (tgv if ot == 0 and half == 0 else [])
                c.mm(PSb[pp][:, half * 512:(half + 1) * 512],
                     wpo[pp][:, k * 128:(k + 1) * 128],
                     avb[:, k * 1024 + half * 512:k * 1024 + (half + 1) * 512],
                     start=(k == 0), stop=False, wait=w_)
            tmm = c.mm(PSb[pp][:, half * 512:(half + 1) * 512],
                       t["brow"][0:1, 1280 + ot * 128:1280 + (ot + 1) * 128],
                       ones_row[0:1, 0:512], start=False, stop=True,
                       wait=[], inc=True)
        mm_last = tmm
        ev_tok[pp] = c.act(lambda e, ot=ot, pp=pp:
                           e.activation(oout[:, ot * 1024:(ot + 1) * 1024],
                                        PSb[pp][:, 0:1024], AF.Copy),
                           wait=[tmm])
    colls, dmas = [], []
    xguard = [None, None]
    ptg = [None, None]
    for j in range(NTT):
        pp = j % 2
        xt = ar(c, O_XT + pp * 8192, [128, 2048], F32)
        tx = c.dma("sync", xt[:, :], t["resid"][j * 128:(j + 1) * 128, :],
                   wait=[xguard[pp]])
        stt_toks = []
        for grp in range(2):
            tT = None
            for kk in range(8):
                dt = grp * 8 + kk
                w_ = []
                if kk == 0:
                    w_ = [ev_tok[0], ev_tok[1], ptg[pp]] + (stt_toks[-1:] if grp == 1 else [])
                tT = c.tr(t["PT0" if pp == 0 else "PT1"][:, kk * 128:(kk + 1) * 128],
                          oout[:, dt * 1024 + j * 128:dt * 1024 + (j + 1) * 128],
                          ident, wait=w_, inc=(kk == 7))
            for kk in range(8):
                dt = grp * 8 + kk
                stt_toks.append(c.dve(
                    lambda e, dt=dt, kk=kk, pp=pp:
                    e.scalar_tensor_tensor(xt[:, dt * 128:(dt + 1) * 128],
                                           c.t["PT0" if pp == 0 else "PT1"][:, kk * 128:(kk + 1) * 128],
                                           1.0,
                                           xt[:, dt * 128:(dt + 1) * 128],
                                           ALU.mult, ALU.add),
                    wait=[tT, tx] if kk == 0 else []))
        ptg[pp] = stt_toks[-1]
        td, trd = ln_tile(c, j, xt[:, :], stt_toks[-8:],
                          t["xln_loc"][j * 128:(j + 1) * 128, :],
                          resid_dst=t["resid"][j * 128:(j + 1) * 128, :])
        dmas += [td, trd]
        xguard[pp] = trd
    c.misc["ln4_dma"] = dmas


def p_flush(c):
    t = c.t
    if DBG_DUMP == "smol":
        xb = ar(c, O_ATT, [128, 2048], F32)
        allw = [(s_, c.cnt[s_]) for s_ in c.sem if s_ not in ("ds",)]
        tz = c.dve(lambda e: e.memset(xb[:, :], 0.0), wait=allw)
        tcp = c.dve(lambda e: e.tensor_copy(xb[:, 0:192], t["smol"][:, :]), wait=[tz])
        tcb = c.dve(lambda e: e.tensor_copy(xb[:, 256:448],
                                            t["junk"][:, 0:192]), wait=[tcp])
        c.dma("sync", t["out_sh"][0:128, :], xb[:, :], wait=[tcb])
        for j in range(1, NTT):
            c.dma("sync", t["out_sh"][j * 128:(j + 1) * 128, :], xb[:, :], wait=[tcb])
        allw2 = [(s_, c.cnt[s_]) for s_ in c.sem]
        c.op("sync", lambda e: None, wait=allw2)
        return
    if DBG_DUMP in ("aac", "qa"):
        src = t["aa_c"] if DBG_DUMP == "aac" else t["q_a"]
        sv = src.rearrange("a b -> (a b)").rearrange("(r p i) -> r p i", p=128, i=2048)
        xb = ar(c, O_ATT, [128, 2048], F32)
        allw = [(s_, c.cnt[s_]) for s_ in c.sem if s_ not in ("ds",)]
        gv = []
        for j in range(NTT):
            tl = c.dma("gpsimd", xb[:, :], sv[j], wait=allw + gv)
            gv = [c.dma("sync", t["out_sh"][j * 128:(j + 1) * 128, :], xb[:, :],
                        wait=[tl])]
        allw2 = [(s_, c.cnt[s_]) for s_ in c.sem]
        c.op("sync", lambda e: None, wait=allw2)
        return
    if DBG_TERN:
        xb = ar(c, O_ATT, [128, 2048], F32)
        gv = []
        for j in range(NTT):
            tl = c.dma("gpsimd", xb[:, :], t["g_wgu"][j * 128:(j + 1) * 128, :],
                       wait=c.misc["w_cc"] + gv)
            gv = [c.dma("sync", t["out_sh"][j * 128:(j + 1) * 128, :], xb[:, :],
                        wait=[tl])]
    elif DBG_PHASE < 9:
        src = t["resid"] if DBG_PHASE >= 4 else t["x_sh"]
        xb = ar(c, O_ATT, [128, 2048], F32)
        gv = []
        prev = list(c.misc.get("ln2_dma", [])) + list(c.misc.get("ln4_dma", []))
        for j in range(NTT):
            tl = c.dma("gpsimd", xb[:, :], src[j * 128:(j + 1) * 128, :],
                       wait=prev + gv)
            gv = [c.dma("sync", t["out_sh"][j * 128:(j + 1) * 128, :], xb[:, :],
                        wait=[tl])]
    allw = [(s, c.cnt[s]) for s in c.sem]
    c.op("sync", lambda e: None, wait=allw)


def program(c):
    p_consts(c)
    p_ternary(c)
    p_ln1(c)
    if DBG_PHASE >= 2:
        p_attn_proj(c, "a")
    if DBG_PHASE >= 3:
        p_attn_a(c)
    if DBG_PHASE >= 4:
        p_resid1_ln2(c)
    if DBG_PHASE >= 5:
        p_ffn(c, 1)
    if DBG_PHASE >= 6:
        p_attn_proj(c, "b")
    if DBG_PHASE >= 7:
        p_attn_b(c)
    if DBG_PHASE >= 8:
        p_oproj(c)
    if DBG_PHASE >= 9:
        p_ffn(c, 2)
    p_flush(c)


def build_nc():
    nc = bass.Bass("TRN2", target_bir_lowering=False)
    with ExitStack() as st:
        t = make_tensors(nc, st)
        sems = {}
        for s in ("ds", "dg", "dc", "cc", "pe", "dv", "ac", "po"):
            sems[s] = st.enter_context(nc.semaphore(s))
        with nc.Block() as block:
            for name in ENGS:
                def runner(e, name=name):
                    ctx = Ctx(nc, t, sems, name, e)
                    program(ctx)
                getattr(block, name)(runner)
    return nc


def kernel(L0_attn_a_x, L0_attn_a_Wq, L0_attn_a_bq, L0_attn_a_Wk, L0_attn_a_bk,
           L0_attn_a_Wv, L0_attn_a_bv, L0_attn_b_w_qkv, L0_attn_b_b_qkv,
           L0_attn_b_w_proj, L0_attn_b_b_proj, L0_ffn_w_gate_up, L0_ffn_w_proj):
    f32 = np.float32
    x = np.ascontiguousarray(np.asarray(L0_attn_a_x, dtype=f32).reshape(S_ALL, E))
    Wq = np.asarray(L0_attn_a_Wq, dtype=f32)
    bq = np.asarray(L0_attn_a_bq, dtype=f32)
    Wk = np.asarray(L0_attn_a_Wk, dtype=f32)
    bk = np.asarray(L0_attn_a_bk, dtype=f32)
    Wv = np.asarray(L0_attn_a_Wv, dtype=f32)
    bv = np.asarray(L0_attn_a_bv, dtype=f32)
    wqkv = np.asarray(L0_attn_b_w_qkv, dtype=f32)
    bqkv = np.asarray(L0_attn_b_b_qkv, dtype=f32)
    wpr = np.asarray(L0_attn_b_w_proj, dtype=f32)
    bpr = np.asarray(L0_attn_b_b_proj, dtype=f32)
    wgu = np.asarray(L0_ffn_w_gate_up, dtype=f32)
    wfp = np.asarray(L0_ffn_w_proj, dtype=f32)

    cst_bf = np.zeros((128, 768), dtype=BF)
    cst_bf[:, 0:128] = np.eye(128, dtype=np.float32).astype(BF)
    cst_bf[:, 256:768] = np.ones((128, 512), dtype=BF)
    cst_f = np.zeros((128, 256), dtype=f32)
    cst_f[:, 0:128] = np.eye(128, dtype=f32)
    tri = np.where(np.arange(128)[None, :] > np.arange(128)[:, None],
                   np.float32(NEG_BIG), np.float32(0.0))
    cst_f[:, 128:256] = tri

    nc = build_nc()
    in_maps = []
    for cid in range(NCORE):
        idxa = np.zeros((128, 64), dtype=np.int32)
        for j in range(8):
            for r in range(8):
                idxa[:, j * 8 + r] = cid * 8192 + r * 1024 + j * 128 + np.arange(128)
        idxb = np.zeros((128, 16), dtype=np.int32)
        for k in range(16):
            idxb[:, k] = cid * 2048 + k * 128 + np.arange(128)
        g = cid // 2
        m = {
            "x_sh": np.ascontiguousarray(x[cid * TOK:(cid + 1) * TOK]),
            "wq_h": np.ascontiguousarray(Wq[2 * cid:2 * cid + 2]),
            "wk_g": np.ascontiguousarray(Wk[g]),
            "wv_g": np.ascontiguousarray(Wv[g]),
            "bq_h": np.ascontiguousarray(bq[2 * cid:2 * cid + 2]),
            "bkv_g": np.ascontiguousarray(np.stack([bk[g], bv[g]])),
            "wqkv_h": np.ascontiguousarray(wqkv[:, 768 * cid:768 * (cid + 1)]),
            "bqkv_h": np.ascontiguousarray(bqkv[768 * cid:768 * (cid + 1)]),
            "wproj": wpr,
            "bproj": bpr,
            "wgu_sh": np.ascontiguousarray(wgu[2048 * cid:2048 * (cid + 1)]),
            "wpr_sh": np.ascontiguousarray(wfp[256 * cid:256 * (cid + 1)]),
            "cst_bf": cst_bf,
            "cst_f": cst_f,
            "idxa": idxa,
            "idxb": idxb,
        }
        in_maps.append(m)
    res = run_bass_kernel_spmd(nc, in_maps, core_ids=list(range(NCORE)))
    out = np.concatenate([np.asarray(res.results[cid]["out_sh"])
                          for cid in range(NCORE)], axis=0)
    return out.reshape(1, S_ALL, E).astype(np.float32)


# version 06
# speedup vs baseline: 1.0586x; 1.0586x over previous
import os
import numpy as np
import ml_dtypes
import concourse.bass as bass
import concourse.mybir as mybir
from concourse.bass_utils import run_bass_kernel_spmd
from contextlib import ExitStack

F32 = mybir.dt.float32
BF16 = mybir.dt.bfloat16
I32 = mybir.dt.int32
U8 = mybir.dt.uint8
AF = mybir.ActivationFunctionType
ALU = mybir.AluOpType
AX = mybir.AxisListType.X
BF = ml_dtypes.bfloat16

NCORE = 8
S_ALL = 8192
E = 2048
TOK = 1024
NTT = 8
DA = 128
DB = 256
HFF = 8192
SCALE_A = 1.0 / float(np.sqrt(DA))
INV_SQ_E = 1.0 / float(np.sqrt(E))
LN_EPS = 1e-5
RMS_EPS = float(np.finfo(np.float32).eps)
THR_C = 0.50048828125
NEG_BIG = -1.0e9

ENGS = ("sync", "gpsimd", "vector", "scalar", "tensor")
DBG_PHASE = int(os.environ.get("MK_DEBUG_PHASE", "99"))
DBG_TERN = int(os.environ.get("MK_DEBUG_TERN", "0"))
DBG_DUMP = os.environ.get("MK_DEBUG_DUMP", "")

O_XT = 0
O_JK = 16384
O_ATT = 32768
O_XLB = 40960
O_C = 49152
O_KV = 126976


class Ctx:
    def __init__(self, nc, t, sems, wname, weng):
        self.nc, self.t, self.sem = nc, t, sems
        self.wname, self.weng = wname, weng
        self.cnt = {k: 0 for k in sems}
        self.waited = {k: 0 for k in sems}
        self.misc = {}

    def op(self, eng, fn, wait=(), inc=None):
        mine = (eng == self.wname)
        if mine:
            for tk in wait:
                if tk is None:
                    continue
                s, v = tk
                if v > self.waited[s]:
                    self.weng.wait_ge(self.sem[s], v)
                    self.waited[s] = v
        inst = fn(self.weng) if mine else None
        if inc is not None:
            s, n = inc
            self.cnt[s] += n
            if mine:
                inst.then_inc(self.sem[s], n)
            return (s, self.cnt[s])
        return None

    def dma(self, q, out, in_, wait=(), **kw):
        s = {"sync": "ds", "gpsimd": "dg", "scalar": "dc"}[q]
        return self.op(q, lambda e: e.dma_start(out=out, in_=in_, **kw), wait, (s, 16))

    def dmat(self, q, out, in_, wait=()):
        s = {"sync": "ds", "scalar": "dc"}[q]
        return self.op(q, lambda e: e.dma_start(out=out, in_=in_, transpose=True),
                       wait, (s, 16))

    def idma(self, out, in_, idx, wait=()):
        return self.op("gpsimd",
                       lambda e: e.indirect_dma_start(
                           out, None, in_, bass.IndirectOffsetOnAxis(ap=idx, axis=0)),
                       wait, ("dg", 16))

    def gather(self, in_ap, out_ap, wait=()):
        return self.op("gpsimd",
                       lambda e: e.collective_compute(
                           "AllGather", ALU.bypass, [list(range(NCORE))],
                           [in_ap], [out_ap]),
                       wait, ("cc", 1))

    def mm(self, out, lhsT, rhs, start, stop, wait=(), inc=False):
        return self.op("tensor",
                       lambda e: e.matmul(out, lhsT, rhs, start=start, stop=stop),
                       wait, ("pe", 1) if inc else None)

    def tr(self, out, in_, ident, wait=(), inc=False):
        return self.op("tensor", lambda e: e.transpose(out, in_, ident),
                       wait, ("pe", 1) if inc else None)

    def act(self, fn, wait=()):
        return self.op("scalar", fn, wait, ("ac", 1))

    def dve(self, fn, wait=()):
        return self.op("vector", fn, wait, ("dv", 1))

    def pool(self, fn, wait=()):
        return self.op("gpsimd", fn, wait, ("po", 1))

    def touch(self, eng, wait=()):
        j = self.t["junk"][:, 0:64]
        j2 = self.t["junk2"][:, 0:64]
        if eng == "scalar":
            return self.act(lambda e: e.activation(j2, j, AF.Copy), wait)
        return self.dve(lambda e: e.tensor_copy(j2, j), wait)


def make_tensors(nc, st):
    t = {}
    dram = lambda n, s, d, k="Internal": nc.dram_tensor(n, s, d, kind=k).ap()
    t["x_sh"] = dram("x_sh", [TOK, E], F32, "ExternalInput")
    t["wq_h"] = dram("wq_h", [2, E, DA], F32, "ExternalInput")
    t["wk_g"] = dram("wk_g", [E, DA], F32, "ExternalInput")
    t["wv_g"] = dram("wv_g", [E, DA], F32, "ExternalInput")
    t["bq_h"] = dram("bq_h", [2, DA], F32, "ExternalInput")
    t["bkv_g"] = dram("bkv_g", [2, DA], F32, "ExternalInput")
    t["wqkv_h"] = dram("wqkv_h", [E, 3 * DB], F32, "ExternalInput")
    t["bqkv_h"] = dram("bqkv_h", [3 * DB], F32, "ExternalInput")
    t["wproj"] = dram("wproj", [E, E], F32, "ExternalInput")
    t["bproj"] = dram("bproj", [E], F32, "ExternalInput")
    t["wgu_sh"] = dram("wgu_sh", [E, E], F32, "ExternalInput")
    t["wpr_sh"] = dram("wpr_sh", [256, HFF], F32, "ExternalInput")
    t["cst_bf"] = dram("cst_bf", [128, 768], BF16, "ExternalInput")
    t["cst_f"] = dram("cst_f", [128, 256], F32, "ExternalInput")
    t["idxa"] = dram("idxa", [128, 64], I32, "ExternalInput")
    t["idxb"] = dram("idxb", [128, 16], I32, "ExternalInput")
    t["out_sh"] = dram("out_sh", [TOK, E], F32, "ExternalOutput")
    t["resid"] = dram("resid", [TOK, E], F32)
    t["xln_c"] = dram("xln_c", [TOK, E], BF16)
    t["xln_loc"] = dram("xln_loc", [TOK, E], BF16)
    t["g_xa"] = dram("g_xa", [S_ALL, E], BF16)
    t["g_xb"] = dram("g_xb", [S_ALL, E], BF16)
    t["q_a"] = dram("q_a", [2 * DA, S_ALL], BF16)
    t["q_b"] = dram("q_b", [DB, S_ALL], BF16)
    t["aa_c"] = dram("aa_c", [S_ALL, 2 * DA], BF16)
    t["g_aa"] = dram("g_aa", [8 * S_ALL, 2 * DA], BF16)
    t["ab_c"] = dram("ab_c", [8 * DB, TOK], BF16)
    t["g_ab"] = dram("g_ab", [8 * 8 * DB, TOK], BF16)
    t["wgu_c"] = dram("wgu_c", [E, E], BF16)
    t["g_wgu"] = dram("g_wgu", [8 * E, E], BF16)
    t["wpr_c"] = dram("wpr_c", [HFF, 256], BF16)
    t["g_wpr"] = dram("g_wpr", [8 * HFF, 256], BF16)
    t["h_d"] = dram("h_d", [HFF, TOK], BF16)
    t["bounce"] = dram("bounce", [1, 2048], F32)
    sb = lambda n, s, d: st.enter_context(nc.sbuf_tensor(n, s, d)).ap()
    t["cbf"] = sb("cbf", [128, 768], BF16)
    t["cf"] = sb("cf", [128, 256], F32)
    t["idxa_s"] = sb("idxa_s", [128, 64], I32)
    t["idxb_s"] = sb("idxb_s", [128, 16], I32)
    t["brow"] = sb("brow", [1, 4096], BF16)
    t["smol"] = sb("smol", [128, 192], F32)
    t["nrow"] = sb("nrow", [1, 512], F32)
    t["junk"] = sb("junk", [128, 2048], F32)
    t["junk2"] = sb("junk2", [128, 2048], F32)
    rem = (nc.sbuf_bytes_remaining - 2048) & ~63
    assert rem >= 156 * 1024 + 1024, f"arena too small: {rem}"
    t["arena"] = sb("arena", [128, rem], U8)
    ps = lambda n, s, d: st.enter_context(nc.psum_tensor(n, s, d)).ap()
    t["PS0"] = ps("PS0", [128, 1024], F32)
    t["PS1"] = ps("PS1", [128, 1024], F32)
    t["PT0"] = ps("PT0", [128, 1024], BF16)
    t["PT1"] = ps("PT1", [128, 1024], BF16)
    t["PA0"] = ps("PA0", [128, 512], F32)
    t["PA1"] = ps("PA1", [128, 512], F32)
    return t


def ar(c, off, shape, dtype):
    esz = {F32: 4, BF16: 2, I32: 4}[dtype]
    v = c.t["arena"][:, off:off + shape[1] * esz].bitcast(dtype)
    if shape[0] != 128:
        v = v[0:shape[0]]
    return v


def smol(c, idx, w=1):
    return c.t["smol"][:, idx:idx + w]


def p_consts(c):
    t = c.t
    tk = [c.dma("sync", t["cbf"][:, :], t["cst_bf"]),
          c.dma("sync", t["cf"][:, :], t["cst_f"]),
          c.dma("sync", t["idxa_s"][:, :], t["idxa"]),
          c.dma("sync", t["idxb_s"][:, :], t["idxb"]),
          c.dma("gpsimd", t["brow"][0:1, 0:256], t["bq_h"].rearrange("a b -> (a b)").unsqueeze(0)),
          c.dma("gpsimd", t["brow"][0:1, 256:512], t["bkv_g"].rearrange("a b -> (a b)").unsqueeze(0)),
          c.dma("gpsimd", t["brow"][0:1, 512:1280], t["bqkv_h"].unsqueeze(0)),
          c.dma("gpsimd", t["brow"][0:1, 1280:3328], t["bproj"].unsqueeze(0))]
    e1 = c.dve(lambda e: e.memset(smol(c, 56), LN_EPS))
    e2 = c.dve(lambda e: e.memset(smol(c, 57), RMS_EPS))
    c.misc["tk_consts"] = tk + [e1, e2]
    wsa = [ar(c, O_C + 53248 + d * 4096, [128, 2048], BF16) for d in range(4)]
    srcs = [t["wq_h"][0], t["wq_h"][1], t["wk_g"], t["wv_g"]]
    c.misc["wa_tok"] = [
        c.dma("gpsimd", wsa[d][:, :].rearrange("p (k i) -> p k i", i=128),
              srcs[d].rearrange("(k p) i -> p k i", p=128)) for d in range(4)]


def p_ternary(c):
    t = c.t
    wbf = [ar(c, O_C + i * 4096, [128, 2048], BF16) for i in range(2)]
    absw = ar(c, O_C + 8192, [128, 2048], BF16)
    sgn = ar(c, O_C + 12288, [128, 2048], BF16)
    cond = ar(c, O_C + 16384, [128, 2048], BF16)
    qq = ar(c, O_C + 20480, [128, 2048], BF16)
    what = ar(c, O_C + 24576, [128, 2048], BF16)
    dd = ar(c, O_C + 28672, [128, 2048], BF16)
    weff = [ar(c, O_C + 32768 + i * 4096, [128, 2048], BF16) for i in range(2)]
    wts = [ar(c, O_C + 40960 + i * 4096, [128, 2048], BF16) for i in range(2)]
    gsum = smol(c, 160, 32)
    thr = smol(c, 128, 32)
    sclb = c.t["smol"][:, 64:80].bitcast(BF16)
    ident = t["cbf"][:, 0:128]
    g = {"tsg": [None, None], "lastT": [None, None], "dma": [None, None],
         "ev": [None, None]}
    prev_tc = prev_tq = prev_td = None
    gathers, gu_done, pr_done = [], [], []
    for u in range(24):
        pp = u % 2
        if u < 16:
            src = t["wgu_sh"][u * 128:(u + 1) * 128, :]
        else:
            rt, cc4 = (u - 16) % 2, (u - 16) // 2
            src = t["wpr_sh"][rt * 128:(rt + 1) * 128, cc4 * 2048:(cc4 + 1) * 2048]
        tl = c.dma("gpsimd", wbf[pp][:, :], src, wait=[g["tsg"][pp]])
        ta = c.act(lambda e, pp=pp: e.activation(absw[:, :], wbf[pp][:, :],
                                                 AF.Abs), wait=[tl, prev_tc])
        a3 = absw[:, :].rearrange("p (g k) -> p g k", k=64)
        tg = c.dve(lambda e: e.reduce_sum(gsum.unsqueeze(2), a3, AX), wait=[ta])
        tsc = c.dve(lambda e: e.tensor_scalar(sclb, gsum, 1.0 / 64, 1e-8,
                                              ALU.mult, ALU.max), wait=[tg])
        tth = c.dve(lambda e: e.tensor_scalar(thr, sclb, THR_C, None, ALU.mult),
                    wait=[tsc])
        th3 = thr.unsqueeze(2).broadcast_to([128, 32, 64])
        c3 = cond[:, :].rearrange("p (g k) -> p g k", k=64)
        tc = c.dve(lambda e: e.tensor_tensor(c3, a3, th3, ALU.is_gt), wait=[tth])
        tsg = c.act(lambda e, pp=pp: e.activation(sgn[:, :], wbf[pp][:, :], AF.Sign),
                    wait=[tl, prev_tq])
        g["tsg"][pp] = tsg
        tq = c.dve(lambda e: e.tensor_tensor(qq[:, :], sgn[:, :], cond[:, :],
                                             ALU.mult), wait=[tc, tsg])
        s3 = sclb.unsqueeze(2).broadcast_to([128, 32, 64])
        q3 = qq[:, :].rearrange("p (g k) -> p g k", k=64)
        w3h = what[:, :].rearrange("p (g k) -> p g k", k=64)
        twh = c.dve(lambda e: e.tensor_tensor(w3h, q3, s3, ALU.mult),
                    wait=[tq, prev_td])
        td = c.pool(lambda e, pp=pp: e.tensor_tensor(dd[:, :], what[:, :],
                                                     wbf[pp][:, :], ALU.subtract),
                    wait=[twh])
        tw = c.pool(lambda e, pp=pp: e.tensor_tensor(weff[pp][:, :], wbf[pp][:, :],
                                                     dd[:, :], ALU.add),
                    wait=[td, g["lastT"][pp]])
        prev_tc, prev_tq, prev_td = tc, tq, td
        ptb = (t["PT0"], t["PT1"])[pp]
        evs = []
        tT = None
        for grp in range(2):
            for k in range(8):
                kk = grp * 8 + k
                w_ = []
                if kk == 0:
                    w_ = [tw, g["ev"][pp]]
                if kk == 8:
                    w_ = [evs[0]]
                tT = c.tr(ptb[:, k * 128:(k + 1) * 128],
                          weff[pp][:, kk * 128:(kk + 1) * 128], ident,
                          wait=w_, inc=(k == 7))
            evs.append(c.dve(lambda e, grp=grp, pp=pp, ptb=ptb:
                             e.tensor_copy(wts[pp][:, grp * 1024:(grp + 1) * 1024],
                                           ptb[:, 0:1024]),
                             wait=[tT, g["dma"][pp]]))
        g["lastT"][pp] = tT
        g["ev"][pp] = evs[1]
        if u < 16:
            dst = t["wgu_c"].rearrange("(k p) i -> p k i", p=128)[:, :, u * 128:(u + 1) * 128]
        else:
            dst = t["wpr_c"][cc4 * 2048:(cc4 + 1) * 2048, rt * 128:(rt + 1) * 128] \
                .rearrange("(k p) i -> p k i", p=128)
        sdt = wts[pp][:, :].rearrange("p (k i) -> p k i", i=128)
        tdm = c.dma("sync", dst, sdt, wait=[evs[1]])
        g["dma"][pp] = tdm
        if u < 16:
            gu_done.append(tdm)
        else:
            pr_done.append(tdm)
    g1 = c.gather(t["wgu_c"][:, :], t["g_wgu"][:, :], wait=gu_done)
    g2 = c.gather(t["wpr_c"][:, :], t["g_wpr"][:, :], wait=pr_done)
    c.misc["w_cc"] = [g1, g2]
    c.misc["tern_pt"] = [g["ev"][0], g["ev"][1]]


def ln_tile(c, j, xt, xt_tok, xln_dst, resid_dst=None, out_dst=None):
    t = c.t
    pp = j % 2
    s1, s2 = smol(c, 0 + pp), smol(c, 2 + pp)
    m, nm2 = smol(c, 4 + pp), smol(c, 6 + pp)
    var, sd = smol(c, 8 + pp), smol(c, 10 + pp)
    rst, bln = smol(c, 12 + pp), smol(c, 14 + pp)
    junkA = ar(c, O_JK, [128, 2048], F32)
    junkB = ar(c, O_JK + 8192, [128, 2048], F32)
    xlb = ar(c, O_XLB + pp * 4096, [128, 2048], BF16)
    t1 = c.dve(lambda e: e.tensor_scalar(junkB[:, :], xt, 1.0, 0.0, ALU.mult,
                                         ALU.add, accum_out=s1), wait=xt_tok)
    t2 = c.act(lambda e: e.activation(junkA[:, :], xt, AF.Square, accum_out=s2),
               wait=xt_tok)
    tm = c.dve(lambda e: e.tensor_scalar(m, s1, 1.0 / E, None, ALU.mult), wait=[t1])
    tn = c.dve(lambda e: e.tensor_scalar(nm2, m, m, -1.0, ALU.mult, ALU.mult),
               wait=[tm])
    tv = c.dve(lambda e: e.scalar_tensor_tensor(var, s2, 1.0 / E, nm2,
                                                ALU.mult, ALU.add), wait=[t2, tn])
    ts_ = c.act(lambda e: e.activation(sd, var, AF.Sqrt, bias=smol(c, 56)),
                wait=[tv] + c.misc["tk_consts"][-2:])
    tr_ = c.dve(lambda e: e.reciprocal(rst, sd), wait=[ts_])
    tb = c.dve(lambda e: e.tensor_scalar(bln, m, rst, -1.0, ALU.mult, ALU.mult),
               wait=[tr_])
    ta = c.act(lambda e: e.activation(xlb[:, :], xt, AF.Identity, bias=bln,
                                      scale=rst), wait=[tb])
    tsl = c.touch("scalar", wait=[ta])
    td = c.dma("sync", xln_dst, xlb[:, :], wait=[tsl])
    trd = None
    if resid_dst is not None:
        trd = c.dma("sync", resid_dst, xt, wait=[ta])
    if out_dst is not None:
        trd = c.dma("sync", out_dst, xt, wait=[ta])
    return td, trd


def p_ln1(c):
    t = c.t
    colls, dmas = [], []
    xguard = [None, None]
    for j in range(NTT):
        pp = j % 2
        xt = ar(c, O_XT + pp * 8192, [128, 2048], F32)
        tx = c.dma("sync", xt[:, :], t["x_sh"][j * 128:(j + 1) * 128, :],
                   wait=[xguard[pp]])
        td, _ = ln_tile(c, j, xt[:, :], [tx], t["xln_c"][j * 128:(j + 1) * 128, :])
        dmas.append(td)
        xguard[pp] = td
        if j % 2 == 1:
            k = j // 2
            colls.append(c.gather(t["xln_c"][k * 256:(k + 1) * 256, :],
                                  t["g_xa"][k * 2048:(k + 1) * 2048, :],
                                  wait=dmas[-2:]))
    c.misc["xa_cc"] = colls


def p_resid1_ln2(c):
    t = c.t
    aa_last = c.misc["aa_cc"][-1]
    dm = []
    xguard = [None, None]
    abguard = [None, None]
    for j in range(NTT):
        pp = j % 2
        xt = ar(c, O_XT + pp * 8192, [128, 2048], F32)
        ab = ar(c, O_ATT + pp * 4096, [128, 2048], BF16)
        tx = c.dma("sync", xt[:, :], t["x_sh"][j * 128:(j + 1) * 128, :],
                   wait=[xguard[pp]])
        tg = []
        for r in range(8):
            tg.append(c.idma(ab[:, r * 256:(r + 1) * 256], t["g_aa"],
                             t["idxa_s"][:, j * 8 + r:j * 8 + r + 1],
                             wait=([aa_last, abguard[pp]] if r == 0 else [])))
        tadd = c.dve(lambda e: e.tensor_tensor(xt, xt, ab[:, :], ALU.add),
                     wait=[tx] + tg)
        abguard[pp] = tadd
        td, trd = ln_tile(c, j, xt[:, :], [tadd],
                          t["xln_loc"][j * 128:(j + 1) * 128, :],
                          resid_dst=t["resid"][j * 128:(j + 1) * 128, :])
        dm += [td, trd]
        xguard[pp] = trd
    c.misc["ln2_dma"] = dm


def p_attn_proj(c, which):
    t = c.t
    g_x = t["g_xa"] if which == "a" else t["g_xb"]
    cc = c.misc["xa_cc"] if which == "a" else c.misc["xb_cc"]
    ident = t["cbf"][:, 0:128]
    ones_row = t["cbf"][0:1, 256:768]
    if which == "a":
        ndout = 4
        ws = [ar(c, O_C + 53248 + d * 4096, [128, 2048], BF16) for d in range(4)]
        rhs = [ar(c, O_C + 16384 + i * 8192, [128, 4096], BF16) for i in range(2)]
        qs0 = [ar(c, O_C + 32768 + i * 512, [128, 256], BF16) for i in range(2)]
        qs1 = [ar(c, O_C + 33792 + i * 512, [128, 256], BF16) for i in range(2)]
        vst = [ar(c, O_C + 34816 + i * 512, [128, 256], BF16) for i in range(2)]
        kg = ar(c, O_KV, [128, 8192], BF16)
        vg = ar(c, O_KV + 16384, [128, 8192], BF16)
        c.misc["kg"], c.misc["vg"] = kg, vg
        brows = [t["brow"][0:1, 0:128], t["brow"][0:1, 128:256],
                 t["brow"][0:1, 256:384], t["brow"][0:1, 384:512]]
        kdst = [kg]
        nv = 2
    else:
        ndout = 6
        vb = ar(c, O_C, [128, 16384], BF16)
        ws = [ar(c, O_C + 32768 + d * 4096, [128, 2048], BF16) for d in range(6)]
        rhs = [ar(c, O_C + 57344 + i * 8192, [128, 4096], BF16) for i in range(2)]
        qs0 = [ar(c, O_C + 73728 + i * 512, [128, 256], BF16) for i in range(2)]
        qs1 = [ar(c, O_C + 74752 + i * 512, [128, 256], BF16) for i in range(2)]
        vst = [ar(c, O_C + 75776 + i * 1024, [128, 512], BF16) for i in range(2)]
        kb = ar(c, O_KV, [128, 16384], BF16)
        c.misc["kb"], c.misc["vb"] = kb, vb
        brows = [t["brow"][0:1, 3328 + d * 128:3328 + (d + 1) * 128] for d in range(6)]
        nv = 4
    wtok = []
    if which == "a":
        wtok = c.misc["wa_tok"]
        bd = []
    else:
        raw = ar(c, O_C, [128, 12288], BF16)
        traw = c.dma("gpsimd", raw[:, :].rearrange("p (k i) -> p k i", i=768),
                     t["wqkv_h"].rearrange("(k p) i -> p k i", p=128))
        for d in range(6):
            sel, half = d // 2, d % 2
            tt = None
            for k in range(16):
                a0 = k * 768 + sel + 3 * half * 128
                src = raw[:, a0:a0 + 3 * 127 + 1:3]
                tt = c.dve(lambda e, d=d, k=k, src=src:
                           e.tensor_copy(ws[d][:, k * 128:(k + 1) * 128], src),
                           wait=[traw] if k == 0 else [])
            wtok.append(tt)
        bd = []
        for d in range(6):
            sel, half = d // 2, d % 2
            a0 = 512 + sel + 3 * half * 128
            src = t["brow"][0:1, a0:a0 + 3 * 127 + 1:3]
            bd.append(c.dve(lambda e, d=d, src=src:
                            e.tensor_copy(t["brow"][0:1, 3328 + d * 128:3328 + (d + 1) * 128],
                                          src),
                            wait=c.misc["tk_consts"][4:8]))
    PS = [t["PS0"][:, 0:512], t["PS0"][:, 512:1024],
          t["PS1"][:, 0:512], t["PS1"][:, 512:1024],
          t["PA0"][:, 0:512], t["PA1"][:, 0:512]]
    evac_tok = [None] * ndout
    mm_tok = [None] * ndout
    pt_guard = list(c.misc["tern_pt"]) if which == "a" else [None, None]
    qdma = []
    last_mm_prev = None
    for tb in range(32):
        pp = tb % 2
        kq, r = tb % 4, tb // 4
        rowb = kq * 2048 + r * 256
        tt = None
        for k in range(16):
            tt = c.dmat("scalar", rhs[pp][:, k * 256:(k + 1) * 256],
                        g_x[rowb:rowb + 256, k * 128:(k + 1) * 128],
                        wait=([cc[kq], last_mm_prev] if k == 0 else []))
        rhs_tok = tt
        for d in range(ndout):
            for k in range(16):
                c.mm(PS[d][:, 0:256], ws[d][:, k * 128:(k + 1) * 128],
                     rhs[pp][:, k * 256:(k + 1) * 256],
                     start=(k == 0), stop=False,
                     wait=([rhs_tok, wtok[d], evac_tok[d]] if k == 0 else []))
            mm_tok[d] = c.mm(PS[d][:, 0:256], brows[d], ones_row[0:1, 0:256],
                             start=False, stop=True,
                             wait=(c.misc["tk_consts"][4:8] + bd) if tb == 0 else [],
                             inc=True)
        last_mm_prev = mm_tok[ndout - 1]
        for d in range(ndout):
            if d < 2:
                dst = (qs0, qs1)[d][pp][:, :]
            elif which == "a" and d == 2:
                dst = c.misc["kg"][:, tb * 256:(tb + 1) * 256]
            elif which == "b" and d in (2, 3):
                dst = c.misc["kb"][:, (d - 2) * 8192 + tb * 256:(d - 2) * 8192 + (tb + 1) * 256]
            elif which == "a":
                dst = vst[pp][:, :]
            else:
                dst = vst[pp][:, (d - 4) * 256:(d - 4) * 256 + 256]
            evac_tok[d] = c.act(lambda e, d=d, dst=dst:
                                e.activation(dst, PS[d][:, 0:256], AF.Copy),
                                wait=[mm_tok[d]])
        qdst = t["q_a"] if which == "a" else t["q_b"]
        t0 = c.dma("sync", qdst[0:128, tb * 256:(tb + 1) * 256], qs0[pp][:, :],
                   wait=[evac_tok[0]])
        t1 = c.dma("sync", qdst[128:256, tb * 256:(tb + 1) * 256], qs1[pp][:, :],
                   wait=[evac_tok[1]])
        qdma += [t0, t1]
        ptb = (t["PT0"], t["PT1"])[pp]
        tT = None
        if which == "a":
            order = [0, 128]
        else:
            order = [0, 256, 128, 384]
        for x, so in enumerate(order):
            w_ = []
            if x == 0:
                w_ = [evac_tok[ndout - 1], evac_tok[ndout - 2], pt_guard[pp]]
            tT = c.tr(ptb[:, x * 128:(x + 1) * 128],
                      vst[pp][:, so:so + 128],
                      ident, wait=w_, inc=(x == nv - 1))
        if which == "a":
            pt_guard[pp] = c.dve(lambda e, tb=tb, ptb=ptb:
                                 e.tensor_copy(c.misc["vg"][:, tb * 256:(tb + 1) * 256],
                                               ptb[:, 0:256]),
                                 wait=[tT])
        else:
            pt_guard[pp] = c.dve(lambda e, tb=tb, ptb=ptb:
                                 e.tensor_copy(c.misc["vb"][:, tb * 512:(tb + 1) * 512],
                                               ptb[:, 0:512]),
                                 wait=[tT])
    c.misc["proj_done_" + which] = [last_mm_prev, pt_guard[0], pt_guard[1]] \
        + list(evac_tok) + qdma[-2:]


def _norm_prep(c, srcs, out_slot, prep_wait):
    t = c.t
    ones_c = t["cbf"][:, 256:257]
    ksq = ar(c, O_C + 36864, [128, 1024], BF16)
    nr = t["nrow"][0:1, 0:512]
    res = smol(c, out_slot)[0:1, :]
    tmp = smol(c, 30)[0:1, :]
    first = True
    tev = None
    tmm = None
    tsq = None
    for get in srcs:
        buf, tok = get([tsq] if tsq else [])
        tsq = c.act(lambda e, buf=buf: e.activation(ksq[:, :], buf, AF.Square),
                    wait=[tok] + (prep_wait if first else []) + ([tmm] if tmm else []))
        for half in range(2):
            tmm = c.mm(t["PA0"][0:1, 0:512], ones_c, ksq[:, half * 512:(half + 1) * 512],
                       start=True, stop=True,
                       wait=[tsq] + ([tev] if tev else []), inc=True)
            tev = c.act(lambda e: e.activation(nr, t["PA0"][0:1, 0:512], AF.Copy),
                        wait=[tmm])
            tevf = c.touch("scalar", wait=[tev])
            trm = c.dve(lambda e: e.reduce_max(tmp, nr, AX), wait=[tevf])
            if first:
                tacc = c.dve(lambda e: e.tensor_copy(res, tmp), wait=[trm])
                first = False
            else:
                tacc = c.dve(lambda e: e.tensor_tensor(res, res, tmp, ALU.max),
                             wait=[trm])
    return tacc


def p_attn_a(c):
    t = c.t
    kg, vg = c.misc["kg"], c.misc["vg"]
    ident = t["cbf"][:, 0:128]
    tri = t["cf"][:, 128:256]
    prep_w = c.misc["proj_done_a"]
    qbuf_n = ar(c, O_C + 38912, [128, 1024], BF16)
    tk2 = _norm_prep(c, [
        (lambda w, ch=ch: (kg[:, ch * 1024:(ch + 1) * 1024], None))
        for ch in range(8)], 31, prep_w)
    q2 = []
    for h in range(2):
        def mk(ch, h=h):
            def get(w):
                tql = c.dma("sync", qbuf_n[:, :],
                            t["q_a"][h * 128:(h + 1) * 128, ch * 1024:(ch + 1) * 1024],
                            wait=w + prep_w)
                return qbuf_n[:, :], tql
            return get
        q2.append(_norm_prep(c, [mk(ch) for ch in range(8)], 32 + h, []))
    k2r = smol(c, 31)[0:1, :]
    negb = [smol(c, 36 + h) for h in range(2)]
    bs = []
    for h in range(2):
        q2r = smol(c, 32 + h)[0:1, :]
        sq1 = smol(c, 168 + 2 * h)[0:1, :]
        sq2 = smol(c, 169 + 2 * h)[0:1, :]
        t1 = c.act(lambda e: e.activation(sq1, k2r, AF.Sqrt), wait=[tk2, q2[1]])
        t2 = c.act(lambda e: e.activation(sq2, q2r, AF.Sqrt), wait=[t1])
        tf = c.touch("scalar", wait=[t2])
        tf = c.touch("scalar", wait=[tf])
        t3 = c.dve(lambda e: e.tensor_tensor(sq1, sq1, sq2, ALU.mult), wait=[tf])
        t4 = c.dve(lambda e: e.tensor_scalar(sq2, sq1, -SCALE_A, None, ALU.mult),
                   wait=[t3])
        t5 = c.touch("vector", wait=[t4])
        t5 = c.touch("vector", wait=[t5])
        t6 = c.dma("sync", t["bounce"][0:1, h:h + 1], sq2, wait=[t5])
        t7 = c.dma("sync", negb[h], t["bounce"][0:1, h:h + 1].broadcast_to([128, 1]),
                   wait=[t6])
        bs.append(t7)
    steps = []
    for tq in range(64):
        nsc = tq // 8 + 1
        for h in range(2):
            for sc in range(nsc):
                L = min(1024, (tq + 1) * 128 - sc * 1024)
                steps.append((tq, h, sc, nsc, L))
    n = len(steps)
    qb = [ar(c, O_C + 40960 + i * 256, [128, 128], BF16) for i in range(2)]
    Pb = [ar(c, O_C + 41984 + i * 2048, [128, 1024], BF16) for i in range(2)]
    PTs = [ar(c, O_C + 46080 + i * 2048, [128, 1024], BF16) for i in range(2)]
    ob = [ar(c, O_C + 50176 + i * 512, [128, 256], BF16) for i in range(2)]
    SB = [t["PS0"], t["PS1"]]
    PTB = [t["PT0"], t["PT1"]]
    AVB = [t["PA0"], t["PA1"]]
    sa = [smol(c, 64, 8), smol(c, 72, 8)]
    sum_fin = [smol(c, 40), smol(c, 41)]
    rfin = [smol(c, 44), smol(c, 45)]
    s_done = [None] * n
    exp_done = [None] * n
    t_done = [None] * n
    ev_done = [None] * n
    av_done = [None] * n
    mask_done = [None] * n
    av_free = [bs[1], bs[1]]
    obuf_free = [None, None]
    fin_guard = [None, None]
    ptg = [prep_w[1], prep_w[2]]
    q_ld = {}
    units = []
    ui = -1
    for i, (tq, h, sc, nsc, L) in enumerate(steps):
        if sc == 0:
            ui += 1
            units.append([ui, tq, h, i, i + nsc - 1])
    step_unit = {}
    for u in units:
        for i in range(u[3], u[4] + 1):
            step_unit[i] = u
    aa_store = {}
    aa_cc = []
    for i in range(n + 3):
        k3 = i - 3
        if 0 <= k3 < n:
            tq, h, sc, nsc, L = steps[k3]
            u = step_unit[k3]
            uu = u[0]
            tAV = None
            for kk in range(L // 128):
                w_ = []
                if kk == 0:
                    w_ = [ev_done[k3]]
                    if sc == 0:
                        w_.append(av_free[uu % 2])
                tAV = c.mm(AVB[uu % 2][:, 0:128],
                           PTs[k3 % 2][:, kk * 128:(kk + 1) * 128],
                           vg[:, (sc * 8 + kk) * 128:(sc * 8 + kk + 1) * 128],
                           start=(sc == 0 and kk == 0),
                           stop=(k3 == u[4] and kk == L // 128 - 1),
                           wait=w_, inc=(kk == L // 128 - 1))
            av_done[k3] = tAV
            if k3 == u[4]:
                tred = c.dve(lambda e, uu=uu, nsc=nsc:
                             e.reduce_sum(sum_fin[uu % 2], sa[uu % 2][:, 0:nsc], AX),
                             wait=[exp_done[k3]])
                fin_guard[uu % 2] = tred
                trc = c.dve(lambda e, uu=uu: e.reciprocal(rfin[uu % 2],
                                                          sum_fin[uu % 2]),
                            wait=[tred, av_free[uu % 2]])
                tf = c.touch("vector", wait=[trc])
                tev = c.act(lambda e, uu=uu:
                            e.activation(ob[uu % 2][:, 0:128], AVB[uu % 2][:, 0:128],
                                         AF.Copy, scale=rfin[uu % 2]),
                            wait=[tAV, tf, obuf_free[uu % 2]])
                av_free[uu % 2] = tev
                tsl = c.touch("scalar", wait=[tev])
                tsl = c.touch("scalar", wait=[tsl])
                tst = c.dma("sync",
                            t["aa_c"][tq * 128:(tq + 1) * 128, h * 128:(h + 1) * 128],
                            ob[uu % 2][:, 0:128], wait=[tsl])
                obuf_free[uu % 2] = tst
                aa_store.setdefault(tq // 8, []).append(tst)
                if tq % 8 == 7 and h == 1:
                    kb_ = tq // 8
                    aa_cc.append(c.gather(
                        t["aa_c"][kb_ * 1024:(kb_ + 1) * 1024, :],
                        t["g_aa"][kb_ * 8192:(kb_ + 1) * 8192, :],
                        wait=aa_store[kb_]))
        j2 = i - 2
        if 0 <= j2 < n:
            tq, h, sc, nsc, L = steps[j2]
            u = step_unit[j2]
            uu = u[0]
            exp_done[j2] = c.act(
                lambda e, j2=j2, L=L, h=h, sc=sc, uu=uu:
                e.activation(Pb[j2 % 2][:, 0:L], SB[j2 % 2][:, 0:L], AF.Exp,
                             bias=negb[h], scale=SCALE_A,
                             accum_out=sa[uu % 2][:, sc:sc + 1]),
                wait=[mask_done[j2] or s_done[j2], bs[h],
                      t_done[j2 - 2] if j2 >= 2 else None,
                      fin_guard[uu % 2] if sc == 0 else None])
            exp_done[j2] = c.touch("scalar", wait=[exp_done[j2]])
        if i < n:
            tq, h, sc, nsc, L = steps[i]
            u = step_unit[i]
            if sc == 0:
                uu = u[0]
                if uu == 0:
                    q_ld[0] = c.dma("sync", qb[0][:, :],
                                    t["q_a"][0:128, 0:128], wait=prep_w)
                if uu + 1 < len(units):
                    nu = units[uu + 1]
                    guard = s_done[units[uu - 1][4]] if uu >= 1 else None
                    q_ld[uu + 1] = c.dma(
                        "sync", qb[(uu + 1) % 2][:, :],
                        t["q_a"][nu[2] * 128:(nu[2] + 1) * 128,
                                 nu[1] * 128:(nu[1] + 1) * 128],
                        wait=[guard])
            tS = None
            for half in range((L + 511) // 512):
                w = min(512, L - half * 512)
                w_ = [q_ld[u[0]], exp_done[i - 2] if i >= 2 else None] \
                    if half == 0 else []
                tS = c.mm(SB[i % 2][:, half * 512:half * 512 + w],
                          qb[u[0] % 2][:, :],
                          kg[:, sc * 1024 + half * 512: sc * 1024 + half * 512 + w],
                          start=True, stop=True,
                          wait=w_, inc=(half == (L - 1) // 512))
            s_done[i] = tS
        j = i - 1
        if 0 <= j < n:
            tq, h, sc, nsc, L = steps[j]
            if sc == nsc - 1:
                mask_done[j] = c.dve(lambda e, j=j, L=L:
                                     e.tensor_tensor(SB[j % 2][:, L - 128:L],
                                                     SB[j % 2][:, L - 128:L],
                                                     tri, ALU.add),
                                     wait=[s_done[j]])
        if 0 <= j2 < n:
            tq, h, sc, nsc, L = steps[j2]
            tT = None
            for kk in range(L // 128):
                w_ = [exp_done[j2], ptg[j2 % 2]] if kk == 0 else []
                tT = c.tr(PTB[j2 % 2][:, kk * 128:(kk + 1) * 128],
                          Pb[j2 % 2][:, kk * 128:(kk + 1) * 128], ident,
                          wait=w_, inc=(kk == L // 128 - 1))
            t_done[j2] = tT
            ev_done[j2] = c.dve(lambda e, j2=j2, L=L:
                                e.tensor_copy(PTs[j2 % 2][:, 0:L], PTB[j2 % 2][:, 0:L]),
                                wait=[tT, av_done[j2 - 2] if j2 >= 2 else None])
            ptg[j2 % 2] = ev_done[j2]
    c.misc["aa_cc"] = aa_cc


def p_attn_b(c):
    t = c.t
    kb, vb = c.misc["kb"], c.misc["vb"]
    ident = t["cbf"][:, 0:128]
    prep_w = c.misc["proj_done_b"]
    n = 64 * 8
    qb = [ar(c, O_C + 32768 + i * 512, [128, 256], BF16) for i in range(2)]
    Pb = [ar(c, O_C + 34816 + i * 2048, [128, 1024], BF16) for i in range(2)]
    PTs = [ar(c, O_C + 38912 + i * 2048, [128, 1024], BF16) for i in range(2)]
    avst = [ar(c, O_C + 43008 + i * 512, [128, 256], BF16) for i in range(2)]
    avT = [ar(c, O_C + 44032 + i * 512, [128, 256], BF16) for i in range(2)]
    SB = [t["PS0"], t["PS1"]]
    PTB = [t["PT0"], t["PT1"]]
    AVB = [t["PA0"], t["PA1"]]
    nm = [smol(c, 46), smol(c, 47)]
    sa = [smol(c, 64, 8), smol(c, 72, 8)]
    rn = [smol(c, 80, 8), smol(c, 88, 8)]
    darr = smol(c, 96, 8)
    earr = smol(c, 104, 8)
    negm_sc = smol(c, 52)
    dn = [smol(c, 53), smol(c, 55)]
    alpha = smol(c, 54)
    sum_fin = [smol(c, 40), smol(c, 41)]
    rfin = [smol(c, 44), smol(c, 45)]
    sfin = [smol(c, 58), smol(c, 59)]
    s_done = [None] * n
    exp_done = [None] * n
    t_done = [None] * n
    ev_done = [None] * n
    av_done = [None] * n
    stat_done = [None] * n
    alpha_done = [None] * n
    resc = [None] * n
    av_free = [prep_w[1], prep_w[2]]
    obuf_free = [None, None]
    fin_guard = [None, None]
    ptg = [prep_w[1], prep_w[2]]
    q_ld = {}
    ab_store = {}
    ab_cc = []
    for i in range(n + 3):
        k3 = i - 3
        if 0 <= k3 < n:
            tq, sc = k3 // 8, k3 % 8
            uu = tq
            tAV = None
            for kk in range(8):
                w_ = []
                if kk == 0:
                    w_ = [ev_done[k3], resc[k3]]
                    if sc == 0:
                        w_.append(av_free[uu % 2])
                tAV = c.mm(AVB[uu % 2][:, 0:256],
                           PTs[k3 % 2][:, kk * 128:(kk + 1) * 128],
                           vb[:, (sc * 8 + kk) * 256:(sc * 8 + kk + 1) * 256],
                           start=(sc == 0 and kk == 0),
                           stop=(sc == 7 and kk == 7),
                           wait=w_, inc=(kk == 7))
            av_done[k3] = tAV
            if sc == 7:
                td_ = c.dve(lambda e, uu=uu:
                            e.tensor_scalar(darr, rn[uu % 2],
                                            rn[uu % 2][:, 7:8], -1.0,
                                            ALU.subtract, ALU.mult),
                            wait=[stat_done[k3]])
                tf = c.touch("vector", wait=[td_])
                te_ = c.act(lambda e: e.activation(earr, darr, AF.Exp), wait=[tf])
                tf2 = c.touch("scalar", wait=[te_])
                tp_ = c.dve(lambda e, uu=uu:
                            e.tensor_tensor(darr, sa[uu % 2], earr, ALU.mult),
                            wait=[tf2, exp_done[k3]])
                fin_guard[uu % 2] = tp_
                tred = c.dve(lambda e, uu=uu:
                             e.reduce_sum(sum_fin[uu % 2], darr, AX),
                             wait=[tp_, av_free[uu % 2]])
                trc = c.dve(lambda e, uu=uu:
                            e.reciprocal(rfin[uu % 2], sum_fin[uu % 2]), wait=[tred])
                tsf = c.dve(lambda e, uu=uu:
                            e.tensor_scalar(sfin[uu % 2], rfin[uu % 2], INV_SQ_E,
                                            None, ALU.mult), wait=[trc])
                tf3 = c.touch("vector", wait=[tsf])
                tev = c.act(lambda e, uu=uu:
                            e.activation(avst[uu % 2][:, :], AVB[uu % 2][:, 0:256],
                                         AF.Copy, scale=sfin[uu % 2]),
                            wait=[tAV, tf3, obuf_free[uu % 2]])
                av_free[uu % 2] = tev
                tT2 = None
                for x in range(2):
                    w_ = [tev, ptg[k3 % 2]] if x == 0 else []
                    tT2 = c.tr(PTB[k3 % 2][:, x * 128:(x + 1) * 128],
                               avst[uu % 2][:, x * 128:(x + 1) * 128], ident,
                               wait=w_, inc=(x == 1))
                tevT = c.dve(lambda e, uu=uu, k3=k3:
                             e.tensor_copy(avT[uu % 2][:, :], PTB[k3 % 2][:, 0:256]),
                             wait=[tT2])
                ptg[k3 % 2] = tevT
                tsl = c.touch("vector", wait=[tevT])
                tsl = c.touch("vector", wait=[tsl])
                tb_ = tq // 8
                qc = (tq % 8) * 128
                dst = t["ab_c"][tb_ * 256:(tb_ + 1) * 256, qc:qc + 128] \
                    .rearrange("(x p) i -> p x i", p=128)
                tst = c.dma("sync", dst,
                            avT[uu % 2][:, :].rearrange("p (x i) -> p x i", x=2),
                            wait=[tsl])
                obuf_free[uu % 2] = tst
                ab_store.setdefault(tb_, []).append(tst)
                if tq % 8 == 7:
                    ab_cc.append(c.gather(
                        t["ab_c"][tb_ * 256:(tb_ + 1) * 256, :],
                        t["g_ab"][tb_ * 2048:(tb_ + 1) * 2048, :],
                        wait=ab_store[tb_]))
        j2 = i - 2
        if 0 <= j2 < n:
            tq, sc = j2 // 8, j2 % 8
            uu = tq
            exp_done[j2] = c.act(
                lambda e, j2=j2, sc=sc, uu=uu:
                e.activation(Pb[j2 % 2][:, 0:1024], SB[j2 % 2][:, 0:1024], AF.Exp,
                             bias=nm[sc % 2], scale=1.0,
                             accum_out=sa[uu % 2][:, sc:sc + 1]),
                wait=[stat_done[j2],
                      t_done[j2 - 2] if j2 >= 2 else None,
                      fin_guard[uu % 2] if sc == 0 else None])
            exp_done[j2] = c.touch("scalar", wait=[exp_done[j2]])
            if sc > 0:
                alpha_done[j2] = c.act(lambda e, sc=sc:
                                       e.activation(alpha, dn[sc % 2], AF.Exp),
                                       wait=[stat_done[j2]])
                resc[j2] = c.act(lambda e, uu=uu:
                                 e.activation(AVB[uu % 2][:, 0:256],
                                              AVB[uu % 2][:, 0:256],
                                              AF.Copy, scale=alpha),
                                 wait=[alpha_done[j2], av_done[j2 - 1]])
        if i < n:
            tq, sc = i // 8, i % 8
            uu = tq
            if sc == 0:
                if uu == 0:
                    q_ld[0] = c.dma("sync",
                                    qb[0][:, :].rearrange("p (a i) -> p a i", i=128),
                                    t["q_b"][:, 0:128].rearrange("(a p) i -> p a i", p=128),
                                    wait=prep_w)
                if uu + 1 < 64:
                    guard = s_done[(uu - 1) * 8 + 7] if uu >= 1 else None
                    q_ld[uu + 1] = c.dma(
                        "sync",
                        qb[(uu + 1) % 2][:, :].rearrange("p (a i) -> p a i", i=128),
                        t["q_b"][:, (uu + 1) * 128:(uu + 2) * 128]
                        .rearrange("(a p) i -> p a i", p=128),
                        wait=[guard])
            tS = None
            for half in range(2):
                for dch in range(2):
                    w_ = [q_ld[uu], exp_done[i - 2] if i >= 2 else None] \
                        if (half == 0 and dch == 0) else []
                    tS = c.mm(SB[i % 2][:, half * 512:(half + 1) * 512],
                              qb[uu % 2][:, dch * 128:(dch + 1) * 128],
                              kb[:, dch * 8192 + sc * 1024 + half * 512:
                                 dch * 8192 + sc * 1024 + (half + 1) * 512],
                              start=(dch == 0), stop=(dch == 1),
                              wait=w_, inc=(half == 1 and dch == 1))
            s_done[i] = tS
        j = i - 1
        if 0 <= j < n:
            tq, sc = j // 8, j % 8
            uu = tq
            tmx = c.dve(lambda e, j=j:
                        e.reduce_max(negm_sc, SB[j % 2][:, 0:1024:4], AX, negate=True),
                        wait=[s_done[j]])
            guard_nm = exp_done[j - 2] if j >= 2 else None
            guard_dn = alpha_done[j - 2] if j >= 2 else None
            if sc == 0:
                tnm = c.dve(lambda e: e.tensor_copy(nm[0], negm_sc),
                            wait=[tmx, guard_nm])
            else:
                tnm = c.dve(lambda e, sc=sc:
                            e.tensor_tensor(nm[sc % 2], nm[(sc - 1) % 2], negm_sc,
                                            ALU.min), wait=[tmx, guard_nm])
                tnm = c.dve(lambda e, sc=sc:
                            e.tensor_tensor(dn[sc % 2], nm[sc % 2], nm[(sc - 1) % 2],
                                            ALU.subtract), wait=[tnm, guard_dn])
            trn_ = c.dve(lambda e, uu=uu, sc=sc:
                         e.tensor_copy(rn[uu % 2][:, sc:sc + 1], nm[sc % 2]),
                         wait=[tnm, fin_guard[uu % 2] if sc == 0 else None])
            stat_done[j] = trn_
        if 0 <= j2 < n:
            tT = None
            for kk in range(8):
                w_ = [exp_done[j2], ptg[j2 % 2]] if kk == 0 else []
                tT = c.tr(PTB[j2 % 2][:, kk * 128:(kk + 1) * 128],
                          Pb[j2 % 2][:, kk * 128:(kk + 1) * 128], ident,
                          wait=w_, inc=(kk == 7))
            t_done[j2] = tT
            ev_done[j2] = c.dve(lambda e, j2=j2:
                                e.tensor_copy(PTs[j2 % 2][:, 0:1024], PTB[j2 % 2][:, 0:1024]),
                                wait=[tT, av_done[j2 - 2] if j2 >= 2 else None])
            ptg[j2 % 2] = ev_done[j2]
    c.misc["ab_cc"] = ab_cc


def p_ffn(c, p):
    t = c.t
    ident = t["cbf"][:, 0:128]
    ones_c = t["cbf"][:, 256:257]
    wcc = c.misc["w_cc"]
    lnw = c.misc["ln2_dma"] if p == 1 else c.misc["ln4_dma"]
    wg = [ar(c, O_C + i * 4096, [128, 2048], BF16) for i in range(2)]
    wu = [ar(c, O_C + 8192 + i * 4096, [128, 2048], BF16) for i in range(2)]
    gs = ar(c, O_C + 16384, [128, 512], F32)
    us = ar(c, O_C + 18432, [128, 512], F32)
    hf = ar(c, O_C + 20480, [128, 512], F32)
    sqb = ar(c, O_C + 22528, [128, 512], BF16)
    hbf = [ar(c, O_C + 23552 + i * 1024, [128, 512], BF16) for i in range(2)]
    s2row = ar(c, O_C + 26624, [1, 1024], F32)
    rstd8 = ar(c, O_C + 30720, [8, 128], F32)
    rstdc = ar(c, O_C + 31232, [128, 8], F32)
    wp = [ar(c, O_C + 32768 + i * 1024, [128, 512], BF16) for i in range(2)]
    hb = [ar(c, O_C + 34816 + i * 2048, [128, 1024], BF16) for i in range(2)]
    ffo = ar(c, O_C + 40960, [128, 16384], BF16)
    xfm = ar(c, O_KV, [128, 16384], BF16)
    txf = None
    for k in range(16):
        txf = c.dmat("scalar", xfm[:, k * 1024:(k + 1) * 1024],
                      t["xln_loc"][0:1024, k * 128:(k + 1) * 128],
                      wait=lnw if k == 0 else [])
    banksG = [t["PS0"][:, 0:512], t["PS1"][:, 0:512]]
    banksU = [t["PS0"][:, 512:1024], t["PS1"][:, 512:1024]]
    banksS = [t["PA0"][0:1, 0:512], t["PA1"][0:1, 0:512]]
    ev_free = [None, None]
    wfree = [None, None]
    hdma = []
    tones = [None, None]
    for ht in range(64):
        pp = ht % 2
        rg, cg = ht // 16, (ht * 128) % 2048
        twg = c.dma("gpsimd",
                    wg[pp][:, :].rearrange("p (k i) -> p k i", i=128),
                    t["g_wgu"][rg * 2048:(rg + 1) * 2048, cg:cg + 128]
                    .rearrange("(k p) i -> p k i", p=128),
                    wait=[wcc[0], wfree[pp]])
        twu = c.dma("gpsimd",
                    wu[pp][:, :].rearrange("p (k i) -> p k i", i=128),
                    t["g_wgu"][(4 + rg) * 2048:(5 + rg) * 2048, cg:cg + 128]
                    .rearrange("(k p) i -> p k i", p=128),
                    wait=[])
        for th in range(2):
            step = (ht * 2 + th) % 2
            tmG = tmU = None
            for k in range(16):
                w_ = [twg, twu, ev_free[step], txf] if k == 0 else []
                tmG = c.mm(banksG[step], wg[pp][:, k * 128:(k + 1) * 128],
                           xfm[:, k * 1024 + th * 512:k * 1024 + (th + 1) * 512],
                           start=(k == 0), stop=(k == 15), wait=w_,
                           inc=(k == 15))
                tmU = c.mm(banksU[step], wu[pp][:, k * 128:(k + 1) * 128],
                           xfm[:, k * 1024 + th * 512:k * 1024 + (th + 1) * 512],
                           start=(k == 0), stop=(k == 15), wait=[],
                           inc=(k == 15))
            tsil = c.act(lambda e, step=step: e.activation(gs[:, :], banksG[step],
                                                           AF.Silu), wait=[tmG])
            tup = c.act(lambda e, step=step: e.activation(us[:, :], banksU[step],
                                                          AF.Copy), wait=[tmU])
            thf = c.dve(lambda e: e.tensor_tensor(hf[:, :], gs[:, :], us[:, :],
                                                  ALU.mult), wait=[tsil, tup])
            ev_free[step] = thf
            tsq = c.act(lambda e: e.activation(sqb[:, :], hf[:, :], AF.Square),
                        wait=[thf])
            tones[th] = c.mm(banksS[th], ones_c, sqb[:, :],
                             start=(ht == 0), stop=(ht == 63),
                             wait=[tsq] + ([wcc[0]] if ht == 0 else []), inc=True)
            thb = c.dve(lambda e, pp2=(ht * 2 + th) % 2:
                        e.tensor_copy(hbf[pp2][:, :], hf[:, :]), wait=[tsq])
            tdh = c.dma("sync", t["h_d"][ht * 128:(ht + 1) * 128,
                                         th * 512:(th + 1) * 512],
                        hbf[(ht * 2 + th) % 2][:, :], wait=[thb])
            hdma.append(tdh)
        wfree[pp] = tmU
    te0 = c.act(lambda e: e.activation(s2row[0:1, 0:512], banksS[0], AF.Copy),
                wait=[tones[0]])
    te1 = c.act(lambda e: e.activation(s2row[0:1, 512:1024], banksS[1], AF.Copy),
                wait=[tones[1]])
    tsd = c.act(lambda e: e.activation(s2row[0:1, :], s2row[0:1, :], AF.Sqrt,
                                       scale=1.0 / HFF, bias=smol(c, 57)[0:1, :]),
                wait=[te0, te1] + c.misc["tk_consts"][-2:])
    trc = c.dve(lambda e: e.reciprocal(s2row[0:1, :], s2row[0:1, :]), wait=[tsd])
    tt_ = c.touch("vector", wait=[trc])
    tbo = c.dma("sync", t["bounce"][0:1, 0:1024], s2row[0:1, :], wait=[tt_])
    tb8 = c.dma("sync", rstd8[0:8, :],
                t["bounce"][0:1, 0:1024].rearrange("o (a b) -> (o a) b", a=8),
                wait=[tbo])
    tT8 = c.tr(t["PA0"][:, 0:8], rstd8[0:8, :], t["cf"][0:8, 0:8],
               wait=[tb8, tones[0], te0, te1], inc=True)
    trsc = c.dve(lambda e: e.tensor_copy(rstdc[:, :], t["PA0"][:, 0:8]), wait=[tT8])
    PT0f = t["PT0"].bitcast(F32)
    PT1f = t["PT1"].bitcast(F32)
    banks8 = [t["PS0"][:, 0:512], t["PS0"][:, 512:1024],
              t["PS1"][:, 0:512], t["PS1"][:, 512:1024],
              t["PA0"][:, 0:512], t["PA1"][:, 0:512],
              PT0f[:, 0:512], PT1f[:, 0:512]]
    ev8 = [None] * 8
    last_pm = None
    for dq in range(4):
        for hc in range(64):
            pp = hc % 2
            cc4, rr = hc // 16, (hc % 16) * 128
            w1 = [last_pm]
            tw1 = c.dma("gpsimd", wp[pp][:, 0:256],
                        t["g_wpr"][(2 * dq) * 8192 + hc * 128:
                                   (2 * dq) * 8192 + hc * 128 + 128, :],
                        wait=[wcc[1]] + w1)
            tw2 = c.dma("gpsimd", wp[pp][:, 256:512],
                        t["g_wpr"][(2 * dq + 1) * 8192 + hc * 128:
                                   (2 * dq + 1) * 8192 + hc * 128 + 128, :],
                        wait=[])
            th_ = c.dma("sync", hb[pp][:, :], t["h_d"][hc * 128:(hc + 1) * 128, :],
                        wait=[hdma[2 * hc + 1] if dq == 0 else hdma[-1], last_pm])
            for j4 in range(4):
                for th in range(2):
                    w_ = []
                    if j4 == 0 and th == 0:
                        w_ = [tw1, tw2, th_]
                        if hc == 0:
                            w_ += [ev8[x] for x in range(8)]
                            w_ += [tT8, trsc]
                    last_pm = c.mm(banks8[j4 * 2 + th],
                                   wp[pp][:, j4 * 128:(j4 + 1) * 128],
                                   hb[pp][:, th * 512:(th + 1) * 512],
                                   start=(hc == 0), stop=(hc == 63),
                                   wait=w_, inc=(j4 == 3 and th == 1))
        for j4 in range(4):
            for th in range(2):
                ev8[j4 * 2 + th] = c.act(
                    lambda e, dq=dq, j4=j4, th=th:
                    e.activation(ffo[:, (dq * 4 + j4) * 1024 + th * 512:
                                      (dq * 4 + j4) * 1024 + (th + 1) * 512],
                                 banks8[j4 * 2 + th], AF.Copy),
                    wait=[last_pm])
    colls, dmas = [], []
    xguard = [None, None]
    ptg = [ev8[6], ev8[7]]
    for j in range(NTT):
        pp = j % 2
        xt = ar(c, O_XT + pp * 8192, [128, 2048], F32)
        tx = c.dma("sync", xt[:, :], t["resid"][j * 128:(j + 1) * 128, :],
                   wait=[xguard[pp]])
        stt_toks = []
        for grp in range(2):
            tT = None
            for kk in range(8):
                dt = grp * 8 + kk
                w_ = []
                if kk == 0:
                    w_ = [ev8[7], ptg[pp]] + (stt_toks[-1:] if grp == 1 else [])
                tT = c.tr(t["PT0" if pp == 0 else "PT1"][:, kk * 128:(kk + 1) * 128],
                          ffo[:, dt * 1024 + j * 128:dt * 1024 + (j + 1) * 128],
                          ident, wait=w_, inc=(kk == 7))
            for kk in range(8):
                dt = grp * 8 + kk
                stt_toks.append(c.dve(
                    lambda e, dt=dt, kk=kk, pp=pp:
                    e.scalar_tensor_tensor(xt[:, dt * 128:(dt + 1) * 128],
                                           c.t["PT0" if pp == 0 else "PT1"][:, kk * 128:(kk + 1) * 128],
                                           rstdc[:, j:j + 1],
                                           xt[:, dt * 128:(dt + 1) * 128],
                                           ALU.mult, ALU.add),
                    wait=[tT, tx, trsc] if kk == 0 else []))
        ptg[pp] = stt_toks[-1]
        if p == 1:
            td, trd = ln_tile(c, j, xt[:, :], stt_toks[-8:],
                              t["xln_c"][j * 128:(j + 1) * 128, :],
                              resid_dst=t["resid"][j * 128:(j + 1) * 128, :])
            dmas.append(td)
            xguard[pp] = trd
            if j % 2 == 1:
                k = j // 2
                colls.append(c.gather(t["xln_c"][k * 256:(k + 1) * 256, :],
                                      t["g_xb"][k * 2048:(k + 1) * 2048, :],
                                      wait=dmas[-2:]))
        else:
            tsl = c.touch("vector", wait=stt_toks[-8:])
            tso = c.dma("sync", t["out_sh"][j * 128:(j + 1) * 128, :], xt[:, :],
                        wait=[tsl] + stt_toks[-8:])
            xguard[pp] = tso
            dmas.append(tso)
    if p == 1:
        c.misc["xb_cc"] = colls
    else:
        c.misc["out_dma"] = dmas


def p_oproj(c):
    t = c.t
    ident = t["cbf"][:, 0:128]
    ones_row = t["cbf"][0:1, 256:768]
    ab_last = c.misc["ab_cc"][-1]
    avb = ar(c, O_C, [128, 16384], BF16)
    oout = ar(c, O_C + 32768, [128, 16384], BF16)
    wpo = [ar(c, O_C + 65536 + i * 4096, [128, 2048], BF16) for i in range(2)]
    tgv = []
    for k in range(16):
        tgv.append(c.idma(avb[:, k * 1024:(k + 1) * 1024], t["g_ab"],
                          t["idxb_s"][:, k:k + 1],
                          wait=[ab_last] if k == 0 else []))
    PSb = [t["PS0"], t["PS1"]]
    ev_tok = [None, None]
    mm_last = None
    for ot in range(16):
        pp = ot % 2
        twp = c.dma("gpsimd",
                    wpo[pp][:, :].rearrange("p (k i) -> p k i", i=128),
                    t["wproj"][:, ot * 128:(ot + 1) * 128]
                    .rearrange("(k p) i -> p k i", p=128),
                    wait=[mm_last] if ot >= 2 else [])
        tmm = None
        for half in range(2):
            for k in range(16):
                w_ = []
                if k == 0:
                    w_ = [twp, ev_tok[pp]] + (tgv if ot == 0 and half == 0 else [])
                c.mm(PSb[pp][:, half * 512:(half + 1) * 512],
                     wpo[pp][:, k * 128:(k + 1) * 128],
                     avb[:, k * 1024 + half * 512:k * 1024 + (half + 1) * 512],
                     start=(k == 0), stop=False, wait=w_)
            tmm = c.mm(PSb[pp][:, half * 512:(half + 1) * 512],
                       t["brow"][0:1, 1280 + ot * 128:1280 + (ot + 1) * 128],
                       ones_row[0:1, 0:512], start=False, stop=True,
                       wait=[], inc=True)
        mm_last = tmm
        ev_tok[pp] = c.act(lambda e, ot=ot, pp=pp:
                           e.activation(oout[:, ot * 1024:(ot + 1) * 1024],
                                        PSb[pp][:, 0:1024], AF.Copy),
                           wait=[tmm])
    colls, dmas = [], []
    xguard = [None, None]
    ptg = [None, None]
    for j in range(NTT):
        pp = j % 2
        xt = ar(c, O_XT + pp * 8192, [128, 2048], F32)
        tx = c.dma("sync", xt[:, :], t["resid"][j * 128:(j + 1) * 128, :],
                   wait=[xguard[pp]])
        stt_toks = []
        for grp in range(2):
            tT = None
            for kk in range(8):
                dt = grp * 8 + kk
                w_ = []
                if kk == 0:
                    w_ = [ev_tok[0], ev_tok[1], ptg[pp]] + (stt_toks[-1:] if grp == 1 else [])
                tT = c.tr(t["PT0" if pp == 0 else "PT1"][:, kk * 128:(kk + 1) * 128],
                          oout[:, dt * 1024 + j * 128:dt * 1024 + (j + 1) * 128],
                          ident, wait=w_, inc=(kk == 7))
            for kk in range(8):
                dt = grp * 8 + kk
                stt_toks.append(c.dve(
                    lambda e, dt=dt, kk=kk, pp=pp:
                    e.scalar_tensor_tensor(xt[:, dt * 128:(dt + 1) * 128],
                                           c.t["PT0" if pp == 0 else "PT1"][:, kk * 128:(kk + 1) * 128],
                                           1.0,
                                           xt[:, dt * 128:(dt + 1) * 128],
                                           ALU.mult, ALU.add),
                    wait=[tT, tx] if kk == 0 else []))
        ptg[pp] = stt_toks[-1]
        td, trd = ln_tile(c, j, xt[:, :], stt_toks[-8:],
                          t["xln_loc"][j * 128:(j + 1) * 128, :],
                          resid_dst=t["resid"][j * 128:(j + 1) * 128, :])
        dmas += [td, trd]
        xguard[pp] = trd
    c.misc["ln4_dma"] = dmas


def p_flush(c):
    t = c.t
    if DBG_DUMP == "smol":
        xb = ar(c, O_ATT, [128, 2048], F32)
        allw = [(s_, c.cnt[s_]) for s_ in c.sem if s_ not in ("ds",)]
        tz = c.dve(lambda e: e.memset(xb[:, :], 0.0), wait=allw)
        tcp = c.dve(lambda e: e.tensor_copy(xb[:, 0:192], t["smol"][:, :]), wait=[tz])
        tcb = c.dve(lambda e: e.tensor_copy(xb[:, 256:448],
                                            t["junk"][:, 0:192]), wait=[tcp])
        c.dma("sync", t["out_sh"][0:128, :], xb[:, :], wait=[tcb])
        for j in range(1, NTT):
            c.dma("sync", t["out_sh"][j * 128:(j + 1) * 128, :], xb[:, :], wait=[tcb])
        allw2 = [(s_, c.cnt[s_]) for s_ in c.sem]
        c.op("sync", lambda e: None, wait=allw2)
        return
    if DBG_DUMP in ("aac", "qa"):
        src = t["aa_c"] if DBG_DUMP == "aac" else t["q_a"]
        sv = src.rearrange("a b -> (a b)").rearrange("(r p i) -> r p i", p=128, i=2048)
        xb = ar(c, O_ATT, [128, 2048], F32)
        allw = [(s_, c.cnt[s_]) for s_ in c.sem if s_ not in ("ds",)]
        gv = []
        for j in range(NTT):
            tl = c.dma("gpsimd", xb[:, :], sv[j], wait=allw + gv)
            gv = [c.dma("sync", t["out_sh"][j * 128:(j + 1) * 128, :], xb[:, :],
                        wait=[tl])]
        allw2 = [(s_, c.cnt[s_]) for s_ in c.sem]
        c.op("sync", lambda e: None, wait=allw2)
        return
    if DBG_TERN:
        xb = ar(c, O_ATT, [128, 2048], F32)
        gv = []
        for j in range(NTT):
            tl = c.dma("gpsimd", xb[:, :], t["g_wgu"][j * 128:(j + 1) * 128, :],
                       wait=c.misc["w_cc"] + gv)
            gv = [c.dma("sync", t["out_sh"][j * 128:(j + 1) * 128, :], xb[:, :],
                        wait=[tl])]
    elif DBG_PHASE < 9:
        src = t["resid"] if DBG_PHASE >= 4 else t["x_sh"]
        xb = ar(c, O_ATT, [128, 2048], F32)
        gv = []
        prev = list(c.misc.get("ln2_dma", [])) + list(c.misc.get("ln4_dma", []))
        for j in range(NTT):
            tl = c.dma("gpsimd", xb[:, :], src[j * 128:(j + 1) * 128, :],
                       wait=prev + gv)
            gv = [c.dma("sync", t["out_sh"][j * 128:(j + 1) * 128, :], xb[:, :],
                        wait=[tl])]
    allw = [(s, c.cnt[s]) for s in c.sem]
    c.op("sync", lambda e: None, wait=allw)


def program(c):
    p_consts(c)
    p_ternary(c)
    p_ln1(c)
    if DBG_PHASE >= 2:
        p_attn_proj(c, "a")
    if DBG_PHASE >= 3:
        p_attn_a(c)
    if DBG_PHASE >= 4:
        p_resid1_ln2(c)
    if DBG_PHASE >= 5:
        p_ffn(c, 1)
    if DBG_PHASE >= 6:
        p_attn_proj(c, "b")
    if DBG_PHASE >= 7:
        p_attn_b(c)
    if DBG_PHASE >= 8:
        p_oproj(c)
    if DBG_PHASE >= 9:
        p_ffn(c, 2)
    p_flush(c)


def build_nc():
    nc = bass.Bass("TRN2", target_bir_lowering=False)
    with ExitStack() as st:
        t = make_tensors(nc, st)
        sems = {}
        for s in ("ds", "dg", "dc", "cc", "pe", "dv", "ac", "po"):
            sems[s] = st.enter_context(nc.semaphore(s))
        with nc.Block() as block:
            for name in ENGS:
                def runner(e, name=name):
                    ctx = Ctx(nc, t, sems, name, e)
                    program(ctx)
                getattr(block, name)(runner)
    return nc


def kernel(L0_attn_a_x, L0_attn_a_Wq, L0_attn_a_bq, L0_attn_a_Wk, L0_attn_a_bk,
           L0_attn_a_Wv, L0_attn_a_bv, L0_attn_b_w_qkv, L0_attn_b_b_qkv,
           L0_attn_b_w_proj, L0_attn_b_b_proj, L0_ffn_w_gate_up, L0_ffn_w_proj):
    f32 = np.float32
    x = np.ascontiguousarray(np.asarray(L0_attn_a_x, dtype=f32).reshape(S_ALL, E))
    Wq = np.asarray(L0_attn_a_Wq, dtype=f32)
    bq = np.asarray(L0_attn_a_bq, dtype=f32)
    Wk = np.asarray(L0_attn_a_Wk, dtype=f32)
    bk = np.asarray(L0_attn_a_bk, dtype=f32)
    Wv = np.asarray(L0_attn_a_Wv, dtype=f32)
    bv = np.asarray(L0_attn_a_bv, dtype=f32)
    wqkv = np.asarray(L0_attn_b_w_qkv, dtype=f32)
    bqkv = np.asarray(L0_attn_b_b_qkv, dtype=f32)
    wpr = np.asarray(L0_attn_b_w_proj, dtype=f32)
    bpr = np.asarray(L0_attn_b_b_proj, dtype=f32)
    wgu = np.asarray(L0_ffn_w_gate_up, dtype=f32)
    wfp = np.asarray(L0_ffn_w_proj, dtype=f32)

    cst_bf = np.zeros((128, 768), dtype=BF)
    cst_bf[:, 0:128] = np.eye(128, dtype=np.float32).astype(BF)
    cst_bf[:, 256:768] = np.ones((128, 512), dtype=BF)
    cst_f = np.zeros((128, 256), dtype=f32)
    cst_f[:, 0:128] = np.eye(128, dtype=f32)
    tri = np.where(np.arange(128)[None, :] > np.arange(128)[:, None],
                   np.float32(NEG_BIG), np.float32(0.0))
    cst_f[:, 128:256] = tri

    nc = build_nc()
    in_maps = []
    for cid in range(NCORE):
        idxa = np.zeros((128, 64), dtype=np.int32)
        for j in range(8):
            for r in range(8):
                idxa[:, j * 8 + r] = cid * 8192 + r * 1024 + j * 128 + np.arange(128)
        idxb = np.zeros((128, 16), dtype=np.int32)
        for k in range(16):
            idxb[:, k] = cid * 2048 + k * 128 + np.arange(128)
        g = cid // 2
        m = {
            "x_sh": np.ascontiguousarray(x[cid * TOK:(cid + 1) * TOK]),
            "wq_h": np.ascontiguousarray(Wq[2 * cid:2 * cid + 2]),
            "wk_g": np.ascontiguousarray(Wk[g]),
            "wv_g": np.ascontiguousarray(Wv[g]),
            "bq_h": np.ascontiguousarray(bq[2 * cid:2 * cid + 2]),
            "bkv_g": np.ascontiguousarray(np.stack([bk[g], bv[g]])),
            "wqkv_h": np.ascontiguousarray(wqkv[:, 768 * cid:768 * (cid + 1)]),
            "bqkv_h": np.ascontiguousarray(bqkv[768 * cid:768 * (cid + 1)]),
            "wproj": wpr,
            "bproj": bpr,
            "wgu_sh": np.ascontiguousarray(wgu[2048 * cid:2048 * (cid + 1)]),
            "wpr_sh": np.ascontiguousarray(wfp[256 * cid:256 * (cid + 1)]),
            "cst_bf": cst_bf,
            "cst_f": cst_f,
            "idxa": idxa,
            "idxb": idxb,
        }
        in_maps.append(m)
    res = run_bass_kernel_spmd(nc, in_maps, core_ids=list(range(NCORE)))
    out = np.concatenate([np.asarray(res.results[cid]["out_sh"])
                          for cid in range(NCORE)], axis=0)
    return out.reshape(1, S_ALL, E).astype(np.float32)
